# version 05
# baseline (speedup 1.0000x reference)
from contextlib import ExitStack

import numpy as np
import concourse.bass as bass
import concourse.mybir as mybir
from concourse.bass_utils import run_bass_kernel_spmd

F32 = mybir.dt.float32
F16 = mybir.dt.float16
I8 = mybir.dt.int8
AF = mybir.ActivationFunctionType
ALU = mybir.AluOpType
AX = mybir.AxisListType

B, S, D, HT, DH, INTER = 4, 4096, 2048, 16, 128, 8192
H = 8
T = 2048
TB = 512
NB = T // TB
NTT = TB // 128
NDC = D // 128
NIS = INTER // 128
SEC = 2048
NSEC = INTER // SEC
ISS = SEC // 128
NQB = S // TB
EPS = 1e-5
INV_SCALE = float(1.0 / np.sqrt(np.float32(128.0)))
GROUPS = [[0, 1], [2, 3], [4, 5], [6, 7]]


class Sem:
    def __init__(self, h):
        self.h = h
        self.n = 0

    def inc(self, amt=1):
        self.n += amt
        return self.n


class Plan:
    ENGINES = ("sync", "gpsimd", "tensor", "vector", "scalar")

    def __init__(self):
        self.q = {e: [] for e in self.ENGINES}

    def add(self, eng, fn):
        self.q[eng].append(fn)

    def wait(self, eng, sem, val):
        if val and val > 0:
            self.add(eng, lambda e, s=sem.h, v=val: e.wait_ge(s, v))

    def dma(self, queue, sem, out, in_):
        v = sem.inc(16)
        self.add(queue, lambda e, o=out, i=in_, s=sem.h:
                 e.dma_start(out=o, in_=i).then_inc(s, 16))
        return v

    def fence(self, queue):
        d, o = self.fsem[queue]
        self.add(queue, lambda e, s=d.h, v=d.n: e.wait_ge(s, v))
        v = o.inc(1)
        self.add(queue, lambda e, s=o.h: e.sem_inc(s, 1))
        return v

    def op(self, eng, fn, sem=None):
        if sem is None:
            self.add(eng, fn)
            return None
        v = sem.inc(1)

        def run(e, f=fn, s=sem.h):
            f(e).then_inc(s, 1)
        self.add(eng, run)
        return v


def build_program(dbg=False):
    nc = bass.Bass("TRN2", target_bir_lowering=False,
                   detect_race_conditions=False)

    x_h = nc.dram_tensor("x_my", [T, D], F32, kind="ExternalInput").ap()
    wg_h = nc.dram_tensor("w_gate", [D, INTER], I8, kind="ExternalInput").ap()
    wu_h = nc.dram_tensor("w_up", [D, INTER], I8, kind="ExternalInput").ap()
    wd_h = nc.dram_tensor("w_down", [INTER, D], I8, kind="ExternalInput").ap()
    sg_h = nc.dram_tensor("s_gate_r", [128, NIS], F32, kind="ExternalInput").ap()
    su_h = nc.dram_tensor("s_up_r", [128, NIS], F32, kind="ExternalInput").ap()
    sd_h = nc.dram_tensor("s_down_r", [128, NDC], F32, kind="ExternalInput").ap()
    wq_h = nc.dram_tensor("wq", [H, D, DH], F32, kind="ExternalInput").ap()
    wk_h = nc.dram_tensor("wk", [H, D, DH], F32, kind="ExternalInput").ap()
    wv_h = nc.dram_tensor("wv", [H, D, DH], F32, kind="ExternalInput").ap()
    wo_h = nc.dram_tensor("wo", [H, DH, D], F32, kind="ExternalInput").ap()
    bq_h = nc.dram_tensor("bq_r", [128, H], F32, kind="ExternalInput").ap()
    bk_h = nc.dram_tensor("bk_r", [128, H], F32, kind="ExternalInput").ap()
    bv_h = nc.dram_tensor("bv_r", [128, H], F32, kind="ExternalInput").ap()
    bo_h = nc.dram_tensor("bo_row", [1, D], F32, kind="ExternalInput").ap()
    masks_h = nc.dram_tensor("masks", [128, 4, TB], F16, kind="ExternalInput").ap()
    ident_h = nc.dram_tensor("ident", [128, 128], F32, kind="ExternalInput").ap()
    ident16_h = nc.dram_tensor("ident16", [128, 128], F16, kind="ExternalInput").ap()
    ones16_h = nc.dram_tensor("ones16", [128, 1], F16, kind="ExternalInput").ap()
    ones32_h = nc.dram_tensor("ones32", [1, 128], F32, kind="ExternalInput").ap()

    y_h = nc.dram_tensor("y_my", [T, D], F32, kind="ExternalOutput").ap()

    wg16 = nc.dram_tensor("wg16", [D, INTER], F16).ap()
    wu16 = nc.dram_tensor("wu16", [D, INTER], F16).ap()
    wd16 = nc.dram_tensor("wd16", [INTER, D], F16).ap()
    x1h = nc.dram_tensor("x1h", [T, D], F32).ap()
    x2h = nc.dram_tensor("x2h", [T, D], F32).ap()
    ln2loc = nc.dram_tensor("ln2loc", [T, D], F32).ap()
    ln2all = nc.dram_tensor("ln2all", [S, D], F32).ap()
    wq1h = nc.dram_tensor("wq1h", [H, D, DH], F16).ap()
    wq2h = nc.dram_tensor("wq2h", [H, D, DH], F16).ap()
    wk1h = nc.dram_tensor("wk1h", [H, D, DH], F16).ap()
    wk2h = nc.dram_tensor("wk2h", [H, D, DH], F16).ap()
    wv1h = nc.dram_tensor("wv1h", [H, D, DH], F16).ap()
    wv2h = nc.dram_tensor("wv2h", [H, D, DH], F16).ap()
    wo1h = nc.dram_tensor("wo1h", [H, DH, D], F16).ap()
    wo2h = nc.dram_tensor("wo2h", [H, DH, D], F16).ap()
    q1h = nc.dram_tensor("q1h", [H, DH, S], F16).ap()
    q2h = nc.dram_tensor("q2h", [H, DH, S], F16).ap()
    k1h = nc.dram_tensor("k1h", [H, DH, S], F16).ap()
    k2h = nc.dram_tensor("k2h", [H, DH, S], F16).ap()
    v1h = nc.dram_tensor("v1h", [H, S, DH], F16).ap()
    v2h = nc.dram_tensor("v2h", [H, S, DH], F16).ap()
    rsin = nc.dram_tensor("rsin", [S, D], F32).ap()
    rsout = nc.dram_tensor("rsout", [T, D], F32).ap()

    P = Plan()

    with ExitStack() as ctx:
        e = ctx.enter_context
        sgate = e(nc.sbuf_tensor("sgate", [128, NIS], F32))
        sup = e(nc.sbuf_tensor("sup", [128, NIS], F32))
        sdown = e(nc.sbuf_tensor("sdown", [128, NDC], F32))
        sdh = e(nc.sbuf_tensor("sdh", [128, NDC], F32))
        bq_s = e(nc.sbuf_tensor("bq_s", [128, H], F32))
        bk_s = e(nc.sbuf_tensor("bk_s", [128, H], F32))
        bv_s = e(nc.sbuf_tensor("bv_s", [128, H], F32))
        bo_bc = e(nc.sbuf_tensor("bo_bc", [128, D], F32))
        masks = e(nc.sbuf_tensor("masks_s", [128, 4, TB], F16))
        ident = e(nc.sbuf_tensor("ident_s", [128, 128], F32))
        ident16 = e(nc.sbuf_tensor("ident16_s", [128, 128], F16))
        ones16 = e(nc.sbuf_tensor("ones16_s", [128, 1], F16))
        ones32 = e(nc.sbuf_tensor("ones32_s", [1, 128], F32))
        ssum = e(nc.sbuf_tensor("ssum", [128, 2, NTT], F32))
        ssq = e(nc.sbuf_tensor("ssq", [128, 2, NTT], F32))
        mcol = e(nc.sbuf_tensor("mcol", [128, 2, NTT], F32))
        vcol = e(nc.sbuf_tensor("vcol", [128, 2, NTT], F32))
        sqc = e(nc.sbuf_tensor("sqc", [128, 2, NTT], F32))
        rstd = e(nc.sbuf_tensor("rstd", [128, 2, NTT], F32))
        epsb = e(nc.sbuf_tensor("epsb", [128, 1], F32))
        negb = e(nc.sbuf_tensor("negb", [128, 1], F32))
        xtA = e(nc.sbuf_tensor("xtA", [128, D], F32))
        xtB = e(nc.sbuf_tensor("xtB", [128, D], F32))
        xn32 = e(nc.sbuf_tensor("xn32", [128, D], F32))
        gub = e(nc.sbuf_tensor("gub", [128, 4, TB], F32))
        acc = e(nc.sbuf_tensor("acc", [128, NDC, TB], F32))
        obuf = e(nc.sbuf_tensor("obuf", [128, D], F32))
        att32 = e(nc.sbuf_tensor("att32", [128, D], F32))
        h16a = e(nc.sbuf_tensor("h16a", [128, 8192], F16))
        h16b = e(nc.sbuf_tensor("h16b", [128, 8192], F16))
        h16c = e(nc.sbuf_tensor("h16c", [128, 8192], F16))
        h16d = e(nc.sbuf_tensor("h16d", [128, 8192], F16))
        h16e = e(nc.sbuf_tensor("h16e", [128, 8192], F16))
        h16f = e(nc.sbuf_tensor("h16f", [128, 8192], F16))
        h16g = e(nc.sbuf_tensor("h16g", [128, 2048], F16))
        h16h = e(nc.sbuf_tensor("h16h", [128, 2048], F16))
        ptok1 = e(nc.sbuf_tensor("ptok1", [128, D], F16))
        ptok2 = e(nc.sbuf_tensor("ptok2", [128, D], F16))

        pb = [e(nc.psum_tensor(f"pb{i}", [128, TB], F32)) for i in range(6)]
        psum1 = e(nc.psum_tensor("psum1", [1, TB], F32))
        pst16 = e(nc.psum_tensor("pst16", [128, 1024], F16))

        LD = Sem(e(nc.semaphore("LD")))
        GD = Sem(e(nc.semaphore("GD")))
        LQ = Sem(e(nc.semaphore("LQ")))
        GQ = Sem(e(nc.semaphore("GQ")))
        AGS = [Sem(e(nc.semaphore(f"AGS{i}"))) for i in range(NB)]
        RSS = [Sem(e(nc.semaphore(f"RSS{i}"))) for i in range(NB)]
        TT = Sem(e(nc.semaphore("TT")))
        VV = Sem(e(nc.semaphore("VV")))
        SC = Sem(e(nc.semaphore("SC")))
        P.fsem = {"sync": (LD, LQ), "gpsimd": (GD, GQ)}

        block = e(nc.Block())

        xnp1 = h16a.ap().rearrange("p (c t) -> p c t", c=NDC)
        xnp2 = h16b.ap().rearrange("p (c t) -> p c t", c=NDC)
        fusep1 = h16c.ap().rearrange("p (c t) -> p c t", c=ISS)
        fusep2 = h16d.ap().rearrange("p (c t) -> p c t", c=ISS)
        wgv = h16e.ap().rearrange("p (r c t) -> p r c t", r=2, c=NDC)
        wuv = h16f.ap().rearrange("p (r c t) -> p r c t", r=2, c=NDC)
        wdv = [h16g.ap().rearrange("p (c t) -> p c t", c=ISS),
               h16h.ap().rearrange("p (c t) -> p c t", c=ISS)]
        qkvw = [h16g.ap().rearrange("p (c t) -> p c t", c=NDC),
                h16h.ap().rearrange("p (c t) -> p c t", c=NDC),
                h16c.ap().rearrange("p (x c t) -> p x c t", x=4, c=NDC)[:, 0],
                h16c.ap().rearrange("p (x c t) -> p x c t", x=4, c=NDC)[:, 1]]
        ksb = [h16a.ap().rearrange("p (w k) -> p w k", w=2),
               h16b.ap().rearrange("p (w k) -> p w k", w=2)]
        vsb = [h16c.ap().rearrange("p (w c d) -> p w c d", w=2, c=32),
               h16d.ap().rearrange("p (w c d) -> p w c d", w=2, c=32)]
        qsb = h16e.ap().rearrange("p (w h t) -> p w h t", w=2, h=H)
        zpv = h16f.ap().rearrange("p (w h t) -> p w h t", w=2, h=H)
        wov = [h16g.ap().rearrange("p (w h t) -> p w h t", w=2, h=H),
               h16h.ap().rearrange("p (w h t) -> p w h t", w=2, h=H)]
        bo_s = obuf.ap().rearrange("p d -> p d")[0:1, :]
        sums32 = gub[0:1, 0, :]
        rcp = gub[0:1, 1, :]
        pt32 = [att32[:, 0:TB], att32[:, TB:2 * TB]]
        zbuf = att32[:, 2 * TB:3 * TB]
        rsbv = att32[:, 3 * TB:4 * TB]
        pst16v = pst16.ap().rearrange("p (j k) -> p j k", j=8)
        ppb = ptok1.ap().rearrange("p (a t) -> p a t", a=4)
        ppflat = ptok1.ap()
        ppx = ptok1.ap()[:, 0:1024].rearrange("p (x j d) -> p x j d", x=2, j=4)

        def t_wait(sem, v):
            P.wait("tensor", sem, v)

        def v_wait(sem, v):
            P.wait("vector", sem, v)

        def s_wait(sem, v):
            P.wait("scalar", sem, v)

        G = {
            "xt": [0, 0],
            "xt_ld": [0, 0],
            "xnp_tt": 0,
            "ptok_tt": 0,
            "ptok_lq": 0,
            "xn32_lq": 0,
            "obuf_mark": [None, None],
            "vv_xnp": 0,
            "acc_tt": 0,
            "wg": [0, 0],
            "wd": [0, 0],
            "gbank": [0, 0],
            "ubank": [0, 0],
            "dbank": [0, 0],
        }

        for dst, src in ((sgate.ap(), bq_h) if False else (sgate.ap(), sg_h),):
            pass
        for dst, src in ((sgate.ap(), sg_h), (sup.ap(), su_h), (sdown.ap(), sd_h),
                         (bq_s.ap(), bq_h), (bk_s.ap(), bk_h), (bv_s.ap(), bv_h),
                         (bo_s, bo_h.rearrange("a d -> a d")), (masks.ap(), masks_h),
                         (ident.ap(), ident_h), (ident16.ap(), ident16_h),
                         (ones16.ap(), ones16_h), (ones32.ap(), ones32_h)):
            P.dma("sync", LD, dst, src)
        f_consts = P.fence("sync")

        v_wait(LQ, f_consts)
        P.op("vector", lambda e: e.tensor_scalar(sdh[:], sdown[:], 32.0, None, ALU.mult))
        P.op("vector", lambda e: e.tensor_scalar(sup[:], sup[:], 0.015625, None, ALU.mult))
        P.op("vector", lambda e: e.memset(epsb[:], float(EPS)))
        P.op("vector", lambda e: e.memset(negb[:], -2.0))

        t_wait(LQ, f_consts)
        for j in range(4):
            tv = P.op("tensor", lambda e, j=j: e.matmul(
                pb[5][:], ones32[:], bo_s[:, j * TB:(j + 1) * TB],
                start=True, stop=True), TT)
            v_wait(TT, tv)
            vv = P.op("vector", lambda e, j=j: e.tensor_copy(
                bo_bc[:, j * TB:(j + 1) * TB], pb[5][:]), VV)
            t_wait(VV, vv)

        def emit_wsplit():
            for (srcw, d1, d2) in ((wq_h, wq1h, wq2h), (wk_h, wk1h, wk2h),
                                   (wv_h, wv1h, wv2h), (wo_h, wo1h, wo2h)):
                flat = srcw.rearrange("h a b -> (h a) b")
                f1 = d1.rearrange("h a b -> (h a) b")
                f2 = d2.rearrange("h a b -> (h a) b")
                nrows, w = flat.shape[0], flat.shape[1]
                for r0 in range(0, nrows, 128):
                    P.wait("sync", VV, G["xt"][0])
                    P.dma("sync", LD, xtA[:, 0:w], flat[r0:r0 + 128, :])
                    fl = P.fence("sync")
                    v_wait(LQ, fl)
                    v_wait(LQ, G["ptok_lq"])
                    v_wait(TT, G["ptok_tt"])
                    P.op("vector", lambda e, w=w: e.tensor_copy(ptok1[:, 0:w], xtA[:, 0:w]))
                    vv = P.op("vector", lambda e, w=w: e.tensor_tensor(
                        ptok2[:, 0:w], xtA[:, 0:w], ptok1[:, 0:w], ALU.subtract), VV)
                    G["xt"][0] = vv
                    P.wait("sync", VV, vv)
                    P.dma("sync", LD, f1[r0:r0 + 128, :], ptok1[:, 0:w])
                    P.dma("sync", LD, f2[r0:r0 + 128, :], ptok2[:, 0:w])
                    G["ptok_lq"] = P.fence("sync")


        def emit_ln_stats(tt, xt, ss, scratch=None):
            scr = scratch if scratch is not None else xn32
            P.op("vector", lambda e, tt=tt, xt=xt, ss=ss: e.reduce_sum(
                ssum[:, ss, tt:tt + 1], xt[:], axis=AX.X))
            P.op("vector", lambda e, xt=xt, scr=scr: e.tensor_tensor(
                scr[:], xt[:], xt[:], ALU.mult))
            return P.op("vector", lambda e, tt=tt, ss=ss, scr=scr: e.reduce_sum(
                ssq[:, ss, tt:tt + 1], scr[:], axis=AX.X), VV)

        def emit_ln_finish(ss):
            v_wait(VV, VV.n)
            a = P.op("vector", lambda e, ss=ss: e.tensor_scalar(
                mcol[:, ss, :], ssum[:, ss, :], 1.0 / D, None, ALU.mult), VV)
            b = P.op("vector", lambda e, ss=ss: e.tensor_scalar(
                vcol[:, ss, :], ssq[:, ss, :], 1.0 / D, None, ALU.mult), VV)
            v_wait(VV, b)
            c = P.op("vector", lambda e, ss=ss: e.tensor_tensor(
                sqc[:, ss, :], mcol[:, ss, :], mcol[:, ss, :], ALU.mult), VV)
            v_wait(VV, c)
            d = P.op("vector", lambda e, ss=ss: e.tensor_tensor(
                vcol[:, ss, :], vcol[:, ss, :], sqc[:, ss, :], ALU.subtract), VV)
            s_wait(VV, d)
            sv = P.op("scalar", lambda e, ss=ss: e.activation(
                out=sqc[:, ss, :], in_=vcol[:, ss, :], func=AF.Sqrt,
                bias=epsb[:], scale=1.0), SC)
            v_wait(SC, sv)
            f = P.op("vector", lambda e, ss=ss: e.reciprocal(
                rstd[:, ss, :], sqc[:, ss, :]), VV)
            v_wait(VV, f)
            return f

        def emit_split_transpose(tt, src32, normalize, dst1, dst2, ss=0):
            if normalize:
                v_wait(LQ, G["xn32_lq"])
                P.op("vector", lambda e, tt=tt, s=src32, ss=ss: e.tensor_scalar(
                    xn32[:], s[:], mcol[:, ss, tt:tt + 1], rstd[:, ss, tt:tt + 1],
                    ALU.subtract, ALU.mult))
                s32 = xn32
            else:
                s32 = src32
            v_wait(TT, G["ptok_tt"])
            v_wait(LQ, G["ptok_lq"])
            P.op("vector", lambda e, s=s32: e.tensor_copy(ptok1[:], s[:]))
            vv = P.op("vector", lambda e, s=s32: e.tensor_tensor(
                ptok2[:], s[:], ptok1[:], ALU.subtract), VV)
            src_done = vv
            t_wait(VV, vv)
            for f in range(4):
                piece, g = f // 2, f % 2
                src = ptok1 if piece == 0 else ptok2
                dst = dst1 if piece == 0 else dst2
                tm = None
                for j in range(8):
                    dc = g * 8 + j
                    tm = P.op("tensor", lambda e, j=j, dc=dc, src=src: e.transpose(
                        pst16[:, j * 128:(j + 1) * 128],
                        src[:, dc * 128:(dc + 1) * 128], ident16[:]),
                        TT if j == 7 else None)
                v_wait(TT, tm)
                vm = P.op("vector", lambda e, dst=dst, g=g, tt=tt: e.tensor_copy(
                    dst[:, g * 8:(g + 1) * 8, tt * 128:(tt + 1) * 128], pst16v[:]), VV)
                t_wait(VV, vm)
            G["ptok_tt"] = TT.n
            return src_done

        def emit_ffn_A(b, xsrc_h, ss):
            for tt in range(NTT):
                xt = xtA if tt % 2 == 0 else xtB
                P.wait("sync", VV, G["xt"][tt % 2])
                P.dma("sync", LD, xt[:],
                      xsrc_h[b * TB + tt * 128: b * TB + (tt + 1) * 128, :])
                fl = P.fence("sync")
                v_wait(LQ, fl)
                G["xt"][tt % 2] = emit_ln_stats(tt, xt, ss)
            emit_ln_finish(ss)
            v_wait(TT, G["xnp_tt"])
            for tt in range(NTT):
                xt = xtA if tt % 2 == 0 else xtB
                P.wait("sync", VV, G["xt"][tt % 2])
                P.dma("sync", LD, xt[:],
                      xsrc_h[b * TB + tt * 128: b * TB + (tt + 1) * 128, :])
                fl = P.fence("sync")
                v_wait(LQ, fl)
                G["xt"][tt % 2] = emit_split_transpose(tt, xt, True, xnp1, xnp2, ss)
            G["vv_xnp"] = VV.n

        def emit_ffn_sections(b, first_pass):
            vv_xnp = G["vv_xnp"]
            for s in range(NSEC):
                if first_pass:
                    c0s, c1s = s * SEC, (s + 1) * SEC
                    for rch in range(4):
                        r0, r1 = rch * 512, (rch + 1) * 512
                        P.dma("gpsimd", GD, wg16[r0:r1, c0s:c1s], wg_h[r0:r1, c0s:c1s])
                        P.dma("gpsimd", GD, wu16[r0:r1, c0s:c1s], wu_h[r0:r1, c0s:c1s])
                    P.dma("gpsimd", GD, wd16[c0s:c1s, :], wd_h[c0s:c1s, :])
                    P.fence("gpsimd")
                for pair in range(8):
                    par = pair % 2
                    c0 = s * SEC + pair * 256
                    P.wait("gpsimd", TT, G["wg"][par])
                    P.dma("gpsimd", GD, wgv[:, par],
                          wg16.rearrange("(c p) i -> p c i", p=128)[:, :, c0:c0 + 256])
                    P.dma("gpsimd", GD, wuv[:, par],
                          wu16.rearrange("(c p) i -> p c i", p=128)[:, :, c0:c0 + 256])
                    f_gu = P.fence("gpsimd")
                    for ii in range(2):
                        isub = s * ISS + pair * 2 + ii
                        si = pair * 2 + ii
                        bg = pb[(si * 2) % 4]
                        bu = pb[(si * 2 + 1) % 4]
                        t_wait(GQ, f_gu)
                        if s == 0 and pair == 0 and ii == 0:
                            t_wait(VV, vv_xnp)
                        t_wait(SC, G["gbank"][si % 2])
                        t_wait(VV, G["ubank"][si % 2])
                        tg = tu = None
                        for dc in range(NDC):
                            for piece in range(2):
                                xp = xnp1 if piece == 0 else xnp2
                                first = (dc == 0 and piece == 0)
                                last = (dc == NDC - 1 and piece == 1)
                                tg = P.op("tensor", lambda e, bg=bg, dc=dc, xp=xp,
                                          par=par, ii=ii, f=first, l=last:
                                          e.matmul(bg[:],
                                                   wgv[:, par, dc, ii * 128:(ii + 1) * 128],
                                                   xp[:, dc, :], start=f, stop=l),
                                          TT if last else None)
                        for dc in range(NDC):
                            for piece in range(2):
                                xp = xnp1 if piece == 0 else xnp2
                                first = (dc == 0 and piece == 0)
                                last = (dc == NDC - 1 and piece == 1)
                                tu = P.op("tensor", lambda e, bu=bu, dc=dc, xp=xp,
                                          par=par, ii=ii, f=first, l=last:
                                          e.matmul(bu[:],
                                                   wuv[:, par, dc, ii * 128:(ii + 1) * 128],
                                                   xp[:, dc, :], start=f, stop=l),
                                          TT if last else None)
                        s_wait(TT, tg)
                        sg_ = P.op("scalar", lambda e, bg=bg, isub=isub, si=si:
                                   e.activation(out=gub[:, (si % 2) * 2, :], in_=bg[:],
                                                func=AF.Silu,
                                                scale=sgate[:, isub:isub + 1]), SC)
                        G["gbank"][si % 2] = sg_
                        v_wait(TT, tu)
                        v_wait(SC, sg_)
                        P.op("vector", lambda e, bu=bu, isub=isub, si=si:
                             e.tensor_scalar(gub[:, (si % 2) * 2 + 1, :], bu[:],
                                             sup[:, isub:isub + 1], None, ALU.mult))
                        P.op("vector", lambda e, si=si: e.tensor_tensor(
                            xn32[:, 0:TB], gub[:, (si % 2) * 2, :],
                            gub[:, (si % 2) * 2 + 1, :], ALU.mult))
                        P.op("vector", lambda e, si=si: e.tensor_copy(
                            fusep1[:, si, :], xn32[:, 0:TB]))
                        fv = P.op("vector", lambda e, si=si: e.tensor_tensor(
                            fusep2[:, si, :], xn32[:, 0:TB], fusep1[:, si, :],
                            ALU.subtract), VV)
                        G["ubank"][si % 2] = fv
                    G["wg"][par] = TT.n
                vv_fuse = VV.n
                for dc in range(NDC):
                    par = dc % 2
                    P.wait("gpsimd", TT, G["wd"][par])
                    P.dma("gpsimd", GD, wdv[par][:],
                          wd16.rearrange("(i p) d -> p i d", p=128)
                          [:, s * ISS:(s + 1) * ISS, dc * 128:(dc + 1) * 128])
                    f_wd = P.fence("gpsimd")
                    bd = pb[4 + (dc % 2)]
                    t_wait(GQ, f_wd)
                    if dc == 0:
                        t_wait(VV, vv_fuse)
                    t_wait(VV, G["dbank"][dc % 2])
                    if s == 0 and dc < 2:
                        t_wait(TT, G["acc_tt"])
                    tm = None
                    for si in range(ISS):
                        for piece in range(2):
                            fp = fusep1 if piece == 0 else fusep2
                            first = (si == 0 and piece == 0)
                            last = (si == ISS - 1 and piece == 1)
                            tm = P.op("tensor", lambda e, bd=bd, si=si, fp=fp, par=par,
                                      f=first, l=last: e.matmul(
                                          bd[:], wdv[par][:, si, :], fp[:, si, :],
                                          start=f, stop=l), TT if last else None)
                    G["wd"][par] = TT.n
                    v_wait(TT, tm)
                    if s == 0:
                        dv = P.op("vector", lambda e, bd=bd, dc=dc: e.tensor_copy(
                            acc[:, dc, :], bd[:]), VV)
                    else:
                        dv = P.op("vector", lambda e, bd=bd, dc=dc: e.tensor_tensor(
                            acc[:, dc, :], acc[:, dc, :], bd[:], ALU.add), VV)
                    G["dbank"][dc % 2] = dv
            G["xnp_tt"] = TT.n

        def emit_ffn_D(b, xsrc_h, dst_h, is_ffn1, ss):
            v_wait(VV, G["dbank"][0])
            v_wait(VV, G["dbank"][1])
            vs_last = None
            for dc in range(NDC):
                vs_last = P.op("vector", lambda e, dc=dc: e.tensor_scalar(
                    acc[:, dc, :], acc[:, dc, :], sdh[:, dc:dc + 1], None, ALU.mult),
                    VV if dc == NDC - 1 else None)
            t_wait(VV, vs_last)
            tpb_g = [0, 0]
            for tt in range(NTT):
                xt = xtA if tt % 2 == 0 else xtB
                ob = obuf if tt % 2 == 0 else att32
                P.wait("sync", VV, G["xt"][tt % 2])
                P.dma("sync", LD, xt[:],
                      xsrc_h[b * TB + tt * 128: b * TB + (tt + 1) * 128, :])
                fl = P.fence("sync")
                vlast = None
                for dg in range(4):
                    bx = pb[dg % 2]
                    t_wait(VV, tpb_g[dg % 2])
                    tm = None
                    for j in range(4):
                        dc = dg * 4 + j
                        tm = P.op("tensor", lambda e, bx=bx, j=j, dc=dc, tt=tt:
                                  e.transpose(bx[:, j * 128:(j + 1) * 128],
                                              acc[:, dc, tt * 128:(tt + 1) * 128],
                                              ident[:]), TT if j == 3 else None)
                    v_wait(TT, tm)
                    v_wait(LQ, fl)
                    if dg == 0 and G["obuf_mark"][tt % 2] is not None:
                        v_wait(LQ, G["obuf_mark"][tt % 2])
                    vlast = P.op("vector", lambda e, bx=bx, dg=dg, xt=xt, ob=ob:
                                 e.tensor_tensor(ob[:, dg * TB:(dg + 1) * TB], bx[:],
                                                 xt[:, dg * TB:(dg + 1) * TB], ALU.add), VV)
                    tpb_g[dg % 2] = vlast
                G["xt"][tt % 2] = vlast
                P.wait("sync", VV, vlast)
                P.dma("sync", LD,
                      dst_h[b * TB + tt * 128: b * TB + (tt + 1) * 128, :], ob[:])
                G["obuf_mark"][tt % 2] = P.fence("sync")
                if is_ffn1:
                    gv = emit_ln_stats(tt, ob, ss, scratch=xn32)
                    G["xt"][tt % 2] = gv
            G["acc_tt"] = TT.n
            if is_ffn1:
                emit_ln_finish(ss)
                for tt in range(NTT):
                    xt = xtA if tt % 2 == 0 else xtB
                    P.wait("sync", VV, G["xt"][tt % 2])
                    P.dma("sync", LD, xt[:],
                          x1h[b * TB + tt * 128: b * TB + (tt + 1) * 128, :])
                    fl = P.fence("sync")
                    v_wait(LQ, fl)
                    v_wait(LQ, G["xn32_lq"])
                    nv = P.op("vector", lambda e, tt=tt, xt=xt, ss=ss: e.tensor_scalar(
                        xn32[:], xt[:], mcol[:, ss, tt:tt + 1], rstd[:, ss, tt:tt + 1],
                        ALU.subtract, ALU.mult), VV)
                    G["xt"][tt % 2] = nv
                    P.wait("sync", VV, nv)
                    P.dma("sync", LD,
                          ln2loc[b * TB + tt * 128: b * TB + (tt + 1) * 128, :],
                          xn32[:])
                f_ln = P.fence("sync")
                G["xn32_lq"] = f_ln
                P.wait("gpsimd", LQ, f_ln)
                P.op("gpsimd", lambda e, b=b: e.collective_compute(
                    "AllGather", ALU.bypass, replica_groups=GROUPS,
                    ins=[ln2loc[b * TB:(b + 1) * TB, :].opt()],
                    outs=[ln2all[b * 2 * TB:(b + 1) * 2 * TB, :].opt()]), AGS[b])

        emit_ffn_A(0, x_h, 0)
        for b_ in range(NB):
            emit_ffn_sections(b_, first_pass=(b_ == 0))
            if b_ == 0:
                emit_wsplit()
            if b_ + 1 < NB:
                emit_ffn_A(b_ + 1, x_h, (b_ + 1) % 2)
            emit_ffn_D(b_, x_h, x1h, True, b_ % 2)

        qkv_w_guard = [TT.n, TT.n]
        pbank_guard = [0, 0, 0]
        for c in range(NB):
            for r in range(2):
                tokbase = r * T + c * TB
                rowbase = c * 2 * TB + r * TB
                t_wait(AGS[c], 1)
                v_wait(AGS[c], 1)
                v_wait(TT, G["xnp_tt"])
                for tt in range(NTT):
                    xt = xtA if tt % 2 == 0 else xtB
                    P.wait("sync", VV, G["xt"][tt % 2])
                    P.wait("sync", AGS[c], 1)
                    P.dma("sync", LD, xt[:],
                          ln2all[rowbase + tt * 128: rowbase + (tt + 1) * 128, :])
                    fl = P.fence("sync")
                    v_wait(LQ, fl)
                    G["xt"][tt % 2] = emit_split_transpose(tt, xt, False, xnp1, xnp2)
                vv_xnp = VV.n
                for h in range(H):
                    for pi, (w1, w2, bias, d1, d2, tokmajor) in enumerate((
                            (wq1h, wq2h, bq_s, q1h, q2h, False),
                            (wk1h, wk2h, bk_s, k1h, k2h, False),
                            (wv1h, wv2h, bv_s, v1h, v2h, True))):
                        par = (h * 3 + pi) % 2
                        P.wait("sync", TT, qkv_w_guard[par])
                        P.dma("sync", LD, qkvw[par * 2][:],
                              w1[h].rearrange("(c p) q -> p c q", p=128))
                        P.dma("sync", LD, qkvw[par * 2 + 1][:],
                              w2[h].rearrange("(c p) q -> p c q", p=128))
                        f_qw = P.fence("sync")
                        bidx = (h * 3 + pi) % 3
                        bk_ = pb[bidx]
                        t_wait(LQ, f_qw)
                        t_wait(VV, vv_xnp)
                        t_wait(VV, pbank_guard[bidx])
                        n_mm = 0
                        tmark = None
                        for dc in range(NDC):
                            for (wv_, xv_) in ((qkvw[par * 2], xnp1), (qkvw[par * 2], xnp2),
                                               (qkvw[par * 2 + 1], xnp1)):
                                n_mm += 1
                                first = n_mm == 1
                                last = n_mm == NDC * 3
                                tmark = P.op("tensor", lambda e, bk_=bk_, wv_=wv_, xv_=xv_,
                                             dc=dc, f=first, l=last: e.matmul(
                                                 bk_[:], wv_[:, dc, :], xv_[:, dc, :],
                                                 start=f, stop=l), TT if last else None)
                        qkv_w_guard[par] = TT.n
                        v_wait(TT, tmark)
                        v_wait(LQ, G["ptok_lq"])
                        v_wait(TT, G["ptok_tt"])
                        P.op("vector", lambda e, bk_=bk_, bias=bias, h=h: e.tensor_scalar(
                            ptok1[:, 0:TB], bk_[:], bias[:, h:h + 1], None, ALU.add))
                        v_wait(LQ, G["xn32_lq"])
                        P.op("vector", lambda e, bk_=bk_, bias=bias, h=h: e.tensor_scalar(
                            xn32[:, 0:TB], bk_[:], bias[:, h:h + 1], None, ALU.add))
                        ev = P.op("vector", lambda e: e.tensor_tensor(
                            ptok2[:, 0:TB], xn32[:, 0:TB], ptok1[:, 0:TB], ALU.subtract), VV)
                        pbank_guard[bidx] = ev
                        if not tokmajor:
                            P.wait("sync", VV, ev)
                            P.dma("sync", LD, d1[h][:, tokbase:tokbase + TB], ptok1[:, 0:TB])
                            P.dma("sync", LD, d2[h][:, tokbase:tokbase + TB],
                                  ptok2[:, 0:TB])
                            G["ptok_lq"] = P.fence("sync")
                        else:
                            t_wait(VV, ev)
                            tm = None
                            for piece in range(2):
                                srcp = ptok1 if piece == 0 else ptok2
                                for j in range(4):
                                    tm = P.op("tensor", lambda e, piece=piece, j=j, srcp=srcp:
                                              e.transpose(
                                                  pst16[:, piece * 512 + j * 128:
                                                        piece * 512 + (j + 1) * 128],
                                                  srcp[:, j * 128:(j + 1) * 128], ident16[:]),
                                              TT if (piece == 1 and j == 3) else None)
                            G["ptok_tt"] = TT.n
                            v_wait(TT, tm)
                            vm = P.op("vector", lambda e: e.tensor_copy(
                                ppflat[:, 0:1024], pst16[:]), VV)
                            t_wait(VV, vm)
                            P.wait("sync", VV, vm)
                            P.dma("sync", LD,
                                  d1[h][tokbase:tokbase + TB, :].rearrange(
                                      "(j p) d -> p j d", p=128), ppx[:, 0])
                            P.dma("sync", LD,
                                  d2[h][tokbase:tokbase + TB, :].rearrange(
                                      "(j p) d -> p j d", p=128), ppx[:, 1])
                            G["ptok_lq"] = P.fence("sync")

        if G["obuf_mark"][1] is not None:
            P.wait("scalar", LQ, G["obuf_mark"][1])
            P.wait("vector", LQ, G["obuf_mark"][1])
        kv_guard = [TT.n, TT.n]
        q_guard = TT.n
        zp_guard = TT.n
        pp_tt_guard = [TT.n, TT.n]
        pt_guard = [VV.n, VV.n]
        psum1_guard = [VV.n]
        zbuf_guard = [VV.n]
        rsb_guard = [VV.n]
        for qb in range(NQB):
            nkc = 4 * (qb + 1)
            ext = nkc * 128
            P.wait("sync", TT, q_guard)
            for h in range(H):
                P.dma("sync", LD, qsb[:, 0, h, :], q1h[h][:, qb * TB:(qb + 1) * TB])
                P.dma("sync", LD, qsb[:, 1, h, :], q2h[h][:, qb * TB:(qb + 1) * TB])
            f_q = P.fence("sync")
            for h in range(H):
                par = h % 2
                P.wait("sync", TT, kv_guard[par])
                P.dma("sync", LD, ksb[par][:, 0, 0:ext], k1h[h][:, 0:ext])
                P.dma("sync", LD, ksb[par][:, 1, 0:ext], k2h[h][:, 0:ext])
                P.dma("sync", LD, vsb[par][:, 0, 0:nkc, :],
                      v1h[h][0:ext, :].rearrange("(c p) d -> p c d", p=128))
                P.dma("sync", LD, vsb[par][:, 1, 0:nkc, :],
                      v2h[h][0:ext, :].rearrange("(c p) d -> p c d", p=128))
                f_kv = P.fence("sync")
                t_wait(LQ, f_kv)
                t_wait(LQ, f_q)
                def emit_scores(kc):
                    slot = kc % 2
                    bS = pb[slot]
                    t_wait(VV, pt_guard[slot])
                    sm = None
                    for (wpi, qpi, f, l) in ((0, 0, True, False), (0, 1, False, False),
                                             (1, 0, False, True)):
                        sm = P.op("tensor", lambda e, bS=bS, par=par, wpi=wpi, qpi=qpi,
                                  kc=kc, h=h, f=f, l=l: e.matmul(
                                      bS[:], ksb[par][:, wpi, kc * 128:(kc + 1) * 128],
                                      qsb[:, qpi, h, :], start=f, stop=l),
                                  TT if l else None)
                    s_wait(TT, sm)
                    s_wait(VV, pt_guard[slot])
                    ex = P.op("scalar", lambda e, bS=bS, slot=slot: e.activation(
                        out=pt32[slot], in_=bS[:], func=AF.Exp, scale=INV_SCALE,
                        bias=negb[:]), SC)
                    v_wait(SC, ex)
                    v_wait(TT, pp_tt_guard[slot])
                    if kc >= qb * 4:
                        j = kc - qb * 4
                        P.op("vector", lambda e, slot=slot, j=j: e.tensor_tensor(
                            pt32[slot], pt32[slot], masks[:, j, :], ALU.mult))
                    if qb == 0 and h == 0 and kc == 0:
                        v_wait(LQ, G["ptok_lq"])
                        v_wait(TT, G["ptok_tt"])
                    P.op("vector", lambda e, slot=slot: e.tensor_copy(
                        ppb[:, slot * 2, :], pt32[slot]))
                    sp = P.op("vector", lambda e, slot=slot: e.tensor_tensor(
                        ppb[:, slot * 2 + 1, :], pt32[slot],
                        ppb[:, slot * 2, :], ALU.subtract), VV)
                    pt_guard[slot] = sp
                    return sp

                def emit_av(kc, sp):
                    slot = kc % 2
                    t_wait(VV, sp)
                    if kc == 0:
                        t_wait(VV, psum1_guard[0])
                        t_wait(VV, zbuf_guard[0])
                    last = kc == nkc - 1
                    P.op("tensor", lambda e, par=par, kc=kc, slot=slot: e.matmul(
                        pb[2][:], vsb[par][:, 0, kc, :], ppb[:, slot * 2, :],
                        start=(kc == 0), stop=False))
                    P.op("tensor", lambda e, par=par, kc=kc, slot=slot: e.matmul(
                        pb[2][:], vsb[par][:, 0, kc, :], ppb[:, slot * 2 + 1, :],
                        start=False, stop=False))
                    P.op("tensor", lambda e, par=par, kc=kc, slot=slot, l=last: e.matmul(
                        pb[2][:], vsb[par][:, 1, kc, :], ppb[:, slot * 2, :],
                        start=False, stop=l))
                    P.op("tensor", lambda e, kc=kc, slot=slot: e.matmul(
                        psum1[:], ones16[:], ppb[:, slot * 2, :],
                        start=(kc == 0), stop=False))
                    tavs = P.op("tensor", lambda e, kc=kc, slot=slot, l=last: e.matmul(
                        psum1[:], ones16[:], ppb[:, slot * 2 + 1, :],
                        start=False, stop=l), TT)
                    pp_tt_guard[slot] = tavs

                pend = None
                for kc in range(nkc):
                    sp = emit_scores(kc)
                    if pend is not None:
                        emit_av(*pend)
                    pend = (kc, sp)
                emit_av(*pend)
                kv_guard[par] = TT.n
                v_wait(TT, TT.n)
                if h == 0:
                    v_wait(TT, zp_guard)
                sc_ = P.op("vector", lambda e: e.tensor_copy(sums32[:], psum1[:]), VV)
                v_wait(VV, sc_)
                rv = P.op("vector", lambda e: e.reciprocal(rcp[:], sums32[:]), VV)
                psum1_guard[0] = rv
                t_wait(VV, rv)
                t_wait(VV, rsb_guard[0])
                bc = P.op("tensor", lambda e: e.matmul(
                    pb[3][:], ones32[:], rcp[:], start=True, stop=True), TT)
                v_wait(TT, bc)
                rsb_guard[0] = P.op("vector", lambda e: e.tensor_copy(rsbv, pb[3][:]), VV)
                zbuf_guard[0] = P.op("vector", lambda e: e.tensor_tensor(
                    zbuf, pb[2][:], rsbv, ALU.mult), VV)
                P.op("vector", lambda e, h=h: e.tensor_copy(zpv[:, 0, h, :], zbuf))
                zv = P.op("vector", lambda e, h=h: e.tensor_tensor(
                    zpv[:, 1, h, :], zbuf, zpv[:, 0, h, :], ALU.subtract), VV)
            wo_guard = [TT.n, TT.n] if qb == 0 else wo_guard
            obank_guard = [0, 0]
            for dmc in range(NDC):
                par = dmc % 2
                P.wait("sync", TT, wo_guard[par])
                P.dma("sync", LD, wov[par][:, 0],
                      wo1h.rearrange("h p d -> p h d")[:, :, dmc * 128:(dmc + 1) * 128])
                P.dma("sync", LD, wov[par][:, 1],
                      wo2h.rearrange("h p d -> p h d")[:, :, dmc * 128:(dmc + 1) * 128])
                f_wo = P.fence("sync")
                bo_ = pb[4 + (dmc % 2)]
                t_wait(LQ, f_wo)
                t_wait(VV, obank_guard[dmc % 2])
                if dmc < 2:
                    t_wait(TT, G["acc_tt"])
                n_mm = 0
                tmark = None
                for h in range(H):
                    for (wpi, zpi) in ((0, 0), (0, 1), (1, 0)):
                        n_mm += 1
                        first = n_mm == 1
                        last = n_mm == H * 3
                        tmark = P.op("tensor", lambda e, bo_=bo_, par=par, wpi=wpi,
                                     zpi=zpi, h=h, f=first, l=last: e.matmul(
                                         bo_[:], wov[par][:, wpi, h, :], zpv[:, zpi, h, :],
                                         start=f, stop=l), TT if last else None)
                wo_guard[par] = TT.n
                v_wait(TT, tmark)
                if dmc < 2:
                    v_wait(TT, G["acc_tt"])
                ov = P.op("vector", lambda e, bo_=bo_, dmc=dmc: e.tensor_copy(
                    acc[:, dmc, :], bo_[:]), VV)
                obank_guard[dmc % 2] = ov
            zp_guard = TT.n
            q_guard = TT.n
            t_wait(SC, SC.n)
            t_wait(VV, VV.n)
            half_g, cc_ = qb // 4, qb % 4
            tpb_g = [0, 0]
            for tt in range(NTT):
                ob = obuf if tt % 2 == 0 else xn32
                vlast = None
                for dg in range(4):
                    bx = pb[dg % 2]
                    t_wait(VV, tpb_g[dg % 2])
                    tm = None
                    for j in range(4):
                        dc = dg * 4 + j
                        tm = P.op("tensor", lambda e, bx=bx, j=j, dc=dc, tt=tt:
                                  e.transpose(bx[:, j * 128:(j + 1) * 128],
                                              acc[:, dc, tt * 128:(tt + 1) * 128],
                                              ident[:]), TT if j == 3 else None)
                    v_wait(TT, tm)
                    if dg == 0:
                        if tt % 2 == 0 and G["obuf_mark"][0] is not None:
                            v_wait(LQ, G["obuf_mark"][0])
                        if tt % 2 == 1:
                            v_wait(LQ, G["xn32_lq"])
                    vlast = P.op("vector", lambda e, bx=bx, dg=dg, ob=ob: e.tensor_copy(
                        ob[:, dg * TB:(dg + 1) * TB], bx[:]), VV)
                    tpb_g[dg % 2] = vlast
                P.wait("sync", VV, vlast)
                srow = cc_ * 2 * TB + half_g * TB + tt * 128
                P.dma("sync", LD, rsin[srow:srow + 128, :], ob[:])
                fmk = P.fence("sync")
                if tt % 2 == 0:
                    G["obuf_mark"][0] = fmk
                else:
                    G["xn32_lq"] = fmk
            G["acc_tt"] = TT.n
            f_stage = P.fence("sync")
            if qb >= 4:
                cdone = qb - 4
                P.wait("gpsimd", LQ, f_stage)
                P.op("gpsimd", lambda e, c=cdone: e.collective_compute(
                    "ReduceScatter", ALU.add, replica_groups=GROUPS,
                    ins=[rsin[c * 2 * TB:(c + 1) * 2 * TB, :].opt()],
                    outs=[rsout[c * TB:(c + 1) * TB, :].opt()]), RSS[cdone])

        G["wg"] = [TT.n, TT.n]
        G["wd"] = [TT.n, TT.n]
        G["xnp_tt"] = TT.n
        G["gbank"] = [SC.n, SC.n]
        G["ubank"] = [VV.n, VV.n]
        G["dbank"] = [VV.n, VV.n]
        def emit_x2_assembly(c):
            v_wait(RSS[c], 1)
            for tt in range(NTT):
                P.wait("sync", RSS[c], 1)
                P.wait("sync", VV, G["xt"][0])
                P.wait("sync", VV, G["xt"][1])
                P.dma("sync", LD, xtA[:],
                      rsout[c * TB + tt * 128: c * TB + (tt + 1) * 128, :])
                P.dma("sync", LD, xtB[:],
                      x1h[c * TB + tt * 128: c * TB + (tt + 1) * 128, :])
                fl = P.fence("sync")
                v_wait(LQ, fl)
                P.op("vector", lambda e: e.tensor_tensor(xtA[:], xtA[:], xtB[:], ALU.add))
                av = P.op("vector", lambda e: e.tensor_tensor(
                    xtA[:], xtA[:], bo_bc[:], ALU.add), VV)
                G["xt"][0] = av
                G["xt"][1] = av
                P.wait("sync", VV, av)
                P.dma("sync", LD, x2h[c * TB + tt * 128: c * TB + (tt + 1) * 128, :], xtA[:])
            P.fence("sync")

        emit_x2_assembly(0)
        emit_ffn_A(0, x2h, 0)
        for c in range(NB):
            emit_ffn_sections(c, first_pass=False)
            if c + 1 < NB:
                emit_x2_assembly(c + 1)
                emit_ffn_A(c + 1, x2h, (c + 1) % 2)
            emit_ffn_D(c, x2h, y_h, False, c % 2)

        if dbg:
            d_x1 = nc.dram_tensor("d_x1", [T, D], F32, kind="ExternalOutput").ap()
            d_l2 = nc.dram_tensor("d_l2", [S, D], F32, kind="ExternalOutput").ap()
            d_k1 = nc.dram_tensor("d_k1", [H, DH, S], F16, kind="ExternalOutput").ap()
            d_ri = nc.dram_tensor("d_ri", [S, D], F32, kind="ExternalOutput").ap()
            d_ro = nc.dram_tensor("d_ro", [T, D], F32, kind="ExternalOutput").ap()
            d_bo = nc.dram_tensor("d_bo", [128, D], F32, kind="ExternalOutput").ap()
            d_x2 = nc.dram_tensor("d_x2", [T, D], F32, kind="ExternalOutput").ap()
            for dst, srcd in ((d_x1, x1h), (d_l2, ln2all), (d_k1, k1h),
                              (d_ri, rsin), (d_ro, rsout), (d_x2, x2h)):
                ss = srcd.rearrange("a b -> (a b)") if len(srcd.shape) == 2 else \
                    srcd.rearrange("a b c -> (a b c)")
                dd = dst.rearrange("a b -> (a b)") if len(dst.shape) == 2 else \
                    dst.rearrange("a b c -> (a b c)")
                n = ss.shape[0]
                step = n // 8
                for i in range(8):
                    P.dma("sync", LD, dd[i * step:(i + 1) * step], ss[i * step:(i + 1) * step])
            P.dma("sync", LD, d_bo[:], bo_bc[:])

        P.fence("sync")
        P.fence("gpsimd")

        @block.sync
        def _(engq):
            for fn in P.q["sync"]:
                fn(engq)

        @block.gpsimd
        def _(engq):
            for fn in P.q["gpsimd"]:
                fn(engq)

        @block.tensor
        def _(engq):
            for fn in P.q["tensor"]:
                fn(engq)

        @block.vector
        def _(engq):
            for fn in P.q["vector"]:
                fn(engq)

        @block.scalar
        def _(engq):
            for fn in P.q["scalar"]:
                fn(engq)

    return nc


def _host_inputs(inputs):
    x = np.ascontiguousarray(np.asarray(inputs["L0_attn_x"], dtype=np.float32))
    WQ = np.asarray(inputs["L0_attn_W_Q"], dtype=np.float32)
    WK = np.asarray(inputs["L0_attn_W_K"], dtype=np.float32)
    WV = np.asarray(inputs["L0_attn_W_V"], dtype=np.float32)
    WO = np.asarray(inputs["L0_attn_W_O"], dtype=np.float32)
    bQ = np.asarray(inputs["L0_attn_b_Q"], dtype=np.float32)
    bK = np.asarray(inputs["L0_attn_b_K"], dtype=np.float32)
    bV = np.asarray(inputs["L0_attn_b_V"], dtype=np.float32)
    bO = np.asarray(inputs["L0_attn_b_O"], dtype=np.float32)
    wg = np.ascontiguousarray(np.asarray(inputs["L0_ffn_w_gate"], dtype=np.int8))
    wu = np.ascontiguousarray(np.asarray(inputs["L0_ffn_w_up"], dtype=np.int8))
    wd = np.ascontiguousarray(np.asarray(inputs["L0_ffn_w_down"], dtype=np.int8))
    sg = np.asarray(inputs["L0_ffn_s_gate"], dtype=np.float32)
    su = np.asarray(inputs["L0_ffn_s_up"], dtype=np.float32)
    sd = np.asarray(inputs["L0_ffn_s_down"], dtype=np.float32)

    sg_r = np.ascontiguousarray(sg.reshape(NIS, 128).T)
    su_r = np.ascontiguousarray(su.reshape(NIS, 128).T)
    sd_r = np.ascontiguousarray(sd.reshape(NDC, 128).T)
    bo_row = np.ascontiguousarray(bO.reshape(1, D))
    jj = np.arange(4)[:, None, None]
    pp_ = np.arange(128)[None, :, None]
    ff = np.arange(TB)[None, None, :]
    masks = (pp_ <= ff - 128 * jj).astype(np.float16)
    masks = np.ascontiguousarray(masks.transpose(1, 0, 2))
    ident = np.eye(128, dtype=np.float32)
    ident16 = np.eye(128, dtype=np.float16)
    ones16 = np.ones((128, 1), dtype=np.float16)
    ones32 = np.ones((1, 128), dtype=np.float32)

    in_maps = []
    for c in range(8):
        b, half = c // 2, c % 2
        hs = slice(half * H, (half + 1) * H)
        in_maps.append({
            "x_my": np.ascontiguousarray(x[b, half * T:(half + 1) * T, :]),
            "w_gate": wg, "w_up": wu, "w_down": wd,
            "s_gate_r": sg_r, "s_up_r": su_r, "s_down_r": sd_r,
            "wq": np.ascontiguousarray(WQ[hs]), "wk": np.ascontiguousarray(WK[hs]),
            "wv": np.ascontiguousarray(WV[hs]), "wo": np.ascontiguousarray(WO[hs]),
            "bq_r": np.ascontiguousarray(bQ[hs].T),
            "bk_r": np.ascontiguousarray(bK[hs].T),
            "bv_r": np.ascontiguousarray(bV[hs].T),
            "bo_row": bo_row, "masks": masks, "ident": ident, "ident16": ident16,
            "ones16": ones16, "ones32": ones32,
        })
    return in_maps


def kernel(**inputs):
    in_maps = _host_inputs(inputs)
    nc = build_program()
    res = run_bass_kernel_spmd(nc, in_maps, core_ids=list(range(8)))
    y = np.empty((B, S, D), dtype=np.float32)
    for c in range(8):
        b, half = c // 2, c % 2
        y[b, half * T:(half + 1) * T, :] = np.asarray(res.results[c]["y_my"])
    return y


# version 11
# speedup vs baseline: 1.2896x; 1.2896x over previous
from contextlib import ExitStack

import numpy as np
import concourse.bass as bass
import concourse.mybir as mybir
from concourse.bass_utils import run_bass_kernel_spmd

F32 = mybir.dt.float32
F16 = mybir.dt.float16
I8 = mybir.dt.int8
AF = mybir.ActivationFunctionType
ALU = mybir.AluOpType
AX = mybir.AxisListType

B, S, D, HT, DH, INTER = 4, 4096, 2048, 16, 128, 8192
H = 8
T = 2048
TB = 512
NB = T // TB
NTT = TB // 128
NDC = D // 128
NIS = INTER // 128
SEC = 2048
NSEC = INTER // SEC
ISS = SEC // 128
NQB = S // TB
EPS = 1e-5
INV_SCALE = float(1.0 / np.sqrt(np.float32(128.0)))
GROUPS = [[0, 1], [2, 3], [4, 5], [6, 7]]


class Sem:
    def __init__(self, h):
        self.h = h
        self.n = 0

    def inc(self, amt=1):
        self.n += amt
        return self.n


class Plan:
    ENGINES = ("sync", "gpsimd", "tensor", "vector", "scalar")

    def __init__(self):
        self.q = {e: [] for e in self.ENGINES}

    def add(self, eng, fn):
        self.q[eng].append(fn)

    def wait(self, eng, sem, val):
        if val and val > 0:
            self.add(eng, lambda e, s=sem.h, v=val: e.wait_ge(s, v))

    def dma(self, queue, sem, out, in_):
        v = sem.inc(16)
        self.add(queue, lambda e, o=out, i=in_, s=sem.h:
                 e.dma_start(out=o, in_=i).then_inc(s, 16))
        return v

    def fence(self, queue):
        d, o = self.fsem[queue]
        self.add(queue, lambda e, s=d.h, v=d.n: e.wait_ge(s, v))
        v = o.inc(1)
        self.add(queue, lambda e, s=o.h: e.sem_inc(s, 1))
        return v

    def op(self, eng, fn, sem=None):
        if sem is None:
            self.add(eng, fn)
            return None
        v = sem.inc(1)

        def run(e, f=fn, s=sem.h):
            f(e).then_inc(s, 1)
        self.add(eng, run)
        return v


def build_program(dbg=False):
    nc = bass.Bass("TRN2", target_bir_lowering=False,
                   detect_race_conditions=False)

    x_h = nc.dram_tensor("x_my", [T, D], F32, kind="ExternalInput").ap()
    wg_h = nc.dram_tensor("w_gate", [D, INTER], I8, kind="ExternalInput").ap()
    wu_h = nc.dram_tensor("w_up", [D, INTER], I8, kind="ExternalInput").ap()
    wd_h = nc.dram_tensor("w_down", [INTER, D], I8, kind="ExternalInput").ap()
    sg_h = nc.dram_tensor("s_gate_r", [128, NIS], F32, kind="ExternalInput").ap()
    su_h = nc.dram_tensor("s_up_r", [128, NIS], F32, kind="ExternalInput").ap()
    sd_h = nc.dram_tensor("s_down_r", [128, NDC], F32, kind="ExternalInput").ap()
    wq_h = nc.dram_tensor("wq", [H, D, DH], F32, kind="ExternalInput").ap()
    wk_h = nc.dram_tensor("wk", [H, D, DH], F32, kind="ExternalInput").ap()
    wv_h = nc.dram_tensor("wv", [H, D, DH], F32, kind="ExternalInput").ap()
    wo_h = nc.dram_tensor("wo", [H, DH, D], F32, kind="ExternalInput").ap()
    bq_h = nc.dram_tensor("bq_r", [128, H], F32, kind="ExternalInput").ap()
    bk_h = nc.dram_tensor("bk_r", [128, H], F32, kind="ExternalInput").ap()
    bv_h = nc.dram_tensor("bv_r", [128, H], F32, kind="ExternalInput").ap()
    bo_h = nc.dram_tensor("bo_r", [128, NDC], F32, kind="ExternalInput").ap()
    masks_h = nc.dram_tensor("masks", [128, 4, TB], F16, kind="ExternalInput").ap()
    ident_h = nc.dram_tensor("ident", [128, 128], F32, kind="ExternalInput").ap()
    ident16_h = nc.dram_tensor("ident16", [128, 128], F16, kind="ExternalInput").ap()
    ones16_h = nc.dram_tensor("ones16", [128, 1], F16, kind="ExternalInput").ap()
    ones32_h = nc.dram_tensor("ones32", [1, 128], F32, kind="ExternalInput").ap()

    y_h = nc.dram_tensor("y_my", [T, D], F32, kind="ExternalOutput").ap()

    wg16 = nc.dram_tensor("wg16", [D, INTER], F16).ap()
    wu16 = nc.dram_tensor("wu16", [D, INTER], F16).ap()
    wd16 = nc.dram_tensor("wd16", [INTER, D], F16).ap()
    x1h = nc.dram_tensor("x1h", [T, D], F32).ap()
    x2h = nc.dram_tensor("x2h", [T, D], F32).ap()
    ln2loc = nc.dram_tensor("ln2loc", [T, D], F32).ap()
    ln2all = nc.dram_tensor("ln2all", [S, D], F32).ap()
    wq1h = nc.dram_tensor("wq1h", [H, D, DH], F16).ap()
    wq2h = nc.dram_tensor("wq2h", [H, D, DH], F16).ap()
    wk1h = nc.dram_tensor("wk1h", [H, D, DH], F16).ap()
    wk2h = nc.dram_tensor("wk2h", [H, D, DH], F16).ap()
    wv1h = nc.dram_tensor("wv1h", [H, D, DH], F16).ap()
    wv2h = nc.dram_tensor("wv2h", [H, D, DH], F16).ap()
    wo1h = nc.dram_tensor("wo1h", [H, DH, D], F16).ap()
    wo2h = nc.dram_tensor("wo2h", [H, DH, D], F16).ap()
    q1h = nc.dram_tensor("q1h", [H, DH, S], F16).ap()
    q2h = nc.dram_tensor("q2h", [H, DH, S], F16).ap()
    k1h = nc.dram_tensor("k1h", [H, DH, S], F16).ap()
    k2h = nc.dram_tensor("k2h", [H, DH, S], F16).ap()
    v1h = nc.dram_tensor("v1h", [H, S, DH], F16).ap()
    v2h = nc.dram_tensor("v2h", [H, S, DH], F16).ap()
    rsin = nc.dram_tensor("rsin", [S, D], F32).ap()
    rsout = nc.dram_tensor("rsout", [T, D], F32).ap()

    P = Plan()

    with ExitStack() as ctx:
        e = ctx.enter_context
        sgate = e(nc.sbuf_tensor("sgate", [128, NIS], F32))
        sup = e(nc.sbuf_tensor("sup", [128, NIS], F32))
        sdown = e(nc.sbuf_tensor("sdown", [128, NDC], F32))
        sdh = e(nc.sbuf_tensor("sdh", [128, NDC], F32))
        bq_s = e(nc.sbuf_tensor("bq_s", [128, H], F32))
        bk_s = e(nc.sbuf_tensor("bk_s", [128, H], F32))
        bv_s = e(nc.sbuf_tensor("bv_s", [128, H], F32))
        bo_r = e(nc.sbuf_tensor("bo_r_s", [128, NDC], F32))
        bo_half = e(nc.sbuf_tensor("bo_half", [128, NDC], F32))
        masks = e(nc.sbuf_tensor("masks_s", [128, 4, TB], F16))
        ident = e(nc.sbuf_tensor("ident_s", [128, 128], F32))
        ident16 = e(nc.sbuf_tensor("ident16_s", [128, 128], F16))
        ones16 = e(nc.sbuf_tensor("ones16_s", [128, 1], F16))
        ones32 = e(nc.sbuf_tensor("ones32_s", [1, 128], F32))
        ssum = e(nc.sbuf_tensor("ssum", [128, 2, NTT], F32))
        ssq = e(nc.sbuf_tensor("ssq", [128, 2, NTT], F32))
        mcol = e(nc.sbuf_tensor("mcol", [128, 2, NTT], F32))
        vcol = e(nc.sbuf_tensor("vcol", [128, 2, NTT], F32))
        sqc = e(nc.sbuf_tensor("sqc", [128, 2, NTT], F32))
        rstd = e(nc.sbuf_tensor("rstd", [128, 2, NTT], F32))
        epsb = e(nc.sbuf_tensor("epsb", [128, 1], F32))
        negb = e(nc.sbuf_tensor("negb", [128, 1], F32))
        xtA = e(nc.sbuf_tensor("xtA", [128, D], F32))
        xtB = e(nc.sbuf_tensor("xtB", [128, D], F32))
        xn32 = e(nc.sbuf_tensor("xn32", [128, D], F32))
        sqscr = e(nc.sbuf_tensor("sqscr", [128, D], F32))
        gub = e(nc.sbuf_tensor("gub", [128, 4, TB], F32))
        acc = e(nc.sbuf_tensor("acc", [128, NDC, TB], F32))
        obuf = e(nc.sbuf_tensor("obuf", [128, D], F32))
        att32 = e(nc.sbuf_tensor("att32", [128, D], F32))
        h16a = e(nc.sbuf_tensor("h16a", [128, 8192], F16))
        h16b = e(nc.sbuf_tensor("h16b", [128, 8192], F16))
        h16c = e(nc.sbuf_tensor("h16c", [128, 8192], F16))
        h16d = e(nc.sbuf_tensor("h16d", [128, 8192], F16))
        h16e = e(nc.sbuf_tensor("h16e", [128, 8192], F16))
        h16f = e(nc.sbuf_tensor("h16f", [128, 8192], F16))
        h16g = e(nc.sbuf_tensor("h16g", [128, 2048], F16))
        h16h = e(nc.sbuf_tensor("h16h", [128, 2048], F16))
        ptok1 = e(nc.sbuf_tensor("ptok1", [128, D], F16))
        ptok2 = e(nc.sbuf_tensor("ptok2", [128, D], F16))

        pb = [e(nc.psum_tensor(f"pb{i}", [128, TB], F32)) for i in range(6)]
        psum1 = e(nc.psum_tensor("psum1", [1, TB], F32))
        pst16 = e(nc.psum_tensor("pst16", [128, 1024], F16))

        LD = Sem(e(nc.semaphore("LD")))
        GD = Sem(e(nc.semaphore("GD")))
        LQ = Sem(e(nc.semaphore("LQ")))
        GQ = Sem(e(nc.semaphore("GQ")))
        AGS = [Sem(e(nc.semaphore(f"AGS{i}"))) for i in range(NB)]
        RSS = [Sem(e(nc.semaphore(f"RSS{i}"))) for i in range(NB)]
        TT = Sem(e(nc.semaphore("TT")))
        VV = Sem(e(nc.semaphore("VV")))
        SC = Sem(e(nc.semaphore("SC")))
        P.fsem = {"sync": (LD, LQ), "gpsimd": (GD, GQ)}

        block = e(nc.Block())

        xnp1 = h16a.ap().rearrange("p (c t) -> p c t", c=NDC)
        xnp2 = h16b.ap().rearrange("p (c t) -> p c t", c=NDC)
        fusep1 = h16c.ap().rearrange("p (c t) -> p c t", c=ISS)
        fusep2 = h16d.ap().rearrange("p (c t) -> p c t", c=ISS)
        wgv = h16e.ap().rearrange("p (r c t) -> p r c t", r=2, c=NDC)
        wuv = h16f.ap().rearrange("p (r c t) -> p r c t", r=2, c=NDC)
        wdv = [h16g.ap().rearrange("p (c t) -> p c t", c=ISS),
               h16h.ap().rearrange("p (c t) -> p c t", c=ISS)]
        qkvw = [h16g.ap().rearrange("p (c t) -> p c t", c=NDC),
                h16h.ap().rearrange("p (c t) -> p c t", c=NDC),
                h16c.ap().rearrange("p (x c t) -> p x c t", x=4, c=NDC)[:, 0],
                h16c.ap().rearrange("p (x c t) -> p x c t", x=4, c=NDC)[:, 1]]
        ksb = [h16a.ap().rearrange("p (w k) -> p w k", w=2),
               h16b.ap().rearrange("p (w k) -> p w k", w=2)]
        vsb = [h16c.ap().rearrange("p (w c d) -> p w c d", w=2, c=32),
               h16d.ap().rearrange("p (w c d) -> p w c d", w=2, c=32)]
        qsb = h16e.ap().rearrange("p (w h t) -> p w h t", w=2, h=H)
        zpv = h16f.ap().rearrange("p (w h t) -> p w h t", w=2, h=H)
        wov = [h16g.ap().rearrange("p (w h t) -> p w h t", w=2, h=H),
               h16h.ap().rearrange("p (w h t) -> p w h t", w=2, h=H)]
        sums32 = gub[0:1, 0, :]
        rcp = gub[0:1, 1, :]
        pt32 = [att32[:, 0:TB], att32[:, TB:2 * TB]]
        zbuf = att32[:, 2 * TB:3 * TB]
        rsbv = att32[:, 3 * TB:4 * TB]
        pst16v = pst16.ap().rearrange("p (j k) -> p j k", j=8)
        ppb = ptok1.ap().rearrange("p (a t) -> p a t", a=4)
        ppflat = ptok1.ap()
        ppx = ptok1.ap()[:, 0:1024].rearrange("p (x j d) -> p x j d", x=2, j=4)

        def t_wait(sem, v):
            P.wait("tensor", sem, v)

        def v_wait(sem, v):
            P.wait("vector", sem, v)

        def s_wait(sem, v):
            P.wait("scalar", sem, v)

        G = {
            "xt": [0, 0],
            "xt_ld": [0, 0],
            "xnp_tt": 0,
            "ptok_tt": 0,
            "ptok_lq": 0,
            "xn32_lq": 0,
            "obuf_mark": [None, None],
            "vv_xnp": 0,
            "acc_tt": 0,
            "wg": [0, 0],
            "wd": [0, 0],
            "gbank": [0, 0],
            "ubank": [0, 0],
            "dbank": [0, 0],
            "pbfree_vv": 0,
        }

        for dst, src in ((sgate.ap(), bq_h) if False else (sgate.ap(), sg_h),):
            pass
        for dst, src in ((sgate.ap(), sg_h), (sup.ap(), su_h), (sdown.ap(), sd_h),
                         (bq_s.ap(), bq_h), (bk_s.ap(), bk_h), (bv_s.ap(), bv_h),
                         (bo_r.ap(), bo_h), (masks.ap(), masks_h),
                         (ident.ap(), ident_h), (ident16.ap(), ident16_h),
                         (ones16.ap(), ones16_h), (ones32.ap(), ones32_h)):
            P.dma("sync", LD, dst, src)
        f_consts = P.fence("sync")

        v_wait(LQ, f_consts)
        P.op("vector", lambda e: e.tensor_scalar(sdh[:], sdown[:], 32.0, None, ALU.mult))
        P.op("vector", lambda e: e.tensor_scalar(bo_half[:], bo_r[:], 0.5, None, ALU.mult))
        P.op("vector", lambda e: e.tensor_scalar(sup[:], sup[:], 0.015625, None, ALU.mult))
        P.op("vector", lambda e: e.memset(epsb[:], float(EPS)))
        P.op("vector", lambda e: e.memset(negb[:], -2.0))

        t_wait(LQ, f_consts)

        def gen_wsplit():
            for (srcw, d1, d2) in ((wq_h, wq1h, wq2h), (wk_h, wk1h, wk2h),
                                   (wv_h, wv1h, wv2h), (wo_h, wo1h, wo2h)):
                flat = srcw.rearrange("h a b -> (h a) b")
                f1 = d1.rearrange("h a b -> (h a) b")
                f2 = d2.rearrange("h a b -> (h a) b")
                nrows, w = flat.shape[0], flat.shape[1]
                rpg = max(1, 1024 // w) * 128
                ng = rpg // 128
                for r0 in range(0, nrows, rpg):
                    wtot = ng * w
                    P.wait("sync", VV, G["xt"][0])
                    P.dma("sync", LD,
                          xtA.ap().rearrange("p (j b) -> p j b", b=w)[:, 0:ng, :],
                          flat[r0:r0 + rpg, :].rearrange("(j p) b -> p j b", p=128))
                    fl = P.fence("sync")
                    v_wait(LQ, fl)
                    v_wait(LQ, G["ptok_lq"])
                    v_wait(TT, G["ptok_tt"])
                    v_wait(VV, VV.n)
                    P.op("vector", lambda e, wtot=wtot: e.tensor_copy(
                        ptok1[:, 0:wtot], xtA[:, 0:wtot]))
                    vv = P.op("vector", lambda e, wtot=wtot: e.tensor_tensor(
                        ptok2[:, 0:wtot], xtA[:, 0:wtot], ptok1[:, 0:wtot],
                        ALU.subtract), VV)
                    G["xt"][0] = vv
                    P.wait("sync", VV, vv)
                    P.dma("sync", LD,
                          f1[r0:r0 + rpg, :].rearrange("(j p) b -> p j b", p=128),
                          ptok1.ap().rearrange("p (j b) -> p j b", b=w)[:, 0:ng, :])
                    P.dma("sync", LD,
                          f2[r0:r0 + rpg, :].rearrange("(j p) b -> p j b", p=128),
                          ptok2.ap().rearrange("p (j b) -> p j b", b=w)[:, 0:ng, :])
                    G["ptok_lq"] = P.fence("sync")
                    yield


        def emit_ln_stats(tt, xt, ss, scratch=None):
            scr = scratch if scratch is not None else sqscr
            v_wait(VV, VV.n)
            P.op("vector", lambda e, tt=tt, xt=xt, ss=ss: e.reduce_sum(
                ssum[:, ss, tt:tt + 1], xt[:], axis=AX.X))
            P.op("vector", lambda e, xt=xt, scr=scr: e.tensor_tensor(
                scr[:], xt[:], xt[:], ALU.mult))
            return P.op("vector", lambda e, tt=tt, ss=ss, scr=scr: e.reduce_sum(
                ssq[:, ss, tt:tt + 1], scr[:], axis=AX.X), VV)

        def emit_ln_finish(ss):
            v_wait(VV, VV.n)
            a = P.op("vector", lambda e, ss=ss: e.tensor_scalar(
                mcol[:, ss, :], ssum[:, ss, :], 1.0 / D, None, ALU.mult), VV)
            b = P.op("vector", lambda e, ss=ss: e.tensor_scalar(
                vcol[:, ss, :], ssq[:, ss, :], 1.0 / D, None, ALU.mult), VV)
            v_wait(VV, b)
            c = P.op("vector", lambda e, ss=ss: e.tensor_tensor(
                sqc[:, ss, :], mcol[:, ss, :], mcol[:, ss, :], ALU.mult), VV)
            v_wait(VV, c)
            d = P.op("vector", lambda e, ss=ss: e.tensor_tensor(
                vcol[:, ss, :], vcol[:, ss, :], sqc[:, ss, :], ALU.subtract), VV)
            s_wait(VV, d)
            sv = P.op("scalar", lambda e, ss=ss: e.activation(
                out=sqc[:, ss, :], in_=vcol[:, ss, :], func=AF.Sqrt,
                bias=epsb[:], scale=1.0), SC)
            v_wait(SC, sv)
            f = P.op("vector", lambda e, ss=ss: e.reciprocal(
                rstd[:, ss, :], sqc[:, ss, :]), VV)
            v_wait(VV, f)
            return f

        def emit_split_transpose(tt, src32, normalize, dst1, dst2, ss=0):
            v_wait(VV, VV.n)
            if normalize:
                v_wait(LQ, G["xn32_lq"])
                P.op("vector", lambda e, tt=tt, s=src32, ss=ss: e.tensor_scalar(
                    xn32[:], s[:], mcol[:, ss, tt:tt + 1], rstd[:, ss, tt:tt + 1],
                    ALU.subtract, ALU.mult))
                s32 = xn32
            else:
                s32 = src32
            v_wait(TT, G["ptok_tt"])
            v_wait(LQ, G["ptok_lq"])
            P.op("vector", lambda e, s=s32: e.tensor_copy(ptok1[:], s[:]))
            vv = P.op("vector", lambda e, s=s32: e.tensor_tensor(
                ptok2[:], s[:], ptok1[:], ALU.subtract), VV)
            src_done = vv
            t_wait(VV, vv)
            for f in range(4):
                piece, g = f // 2, f % 2
                src = ptok1 if piece == 0 else ptok2
                dst = dst1 if piece == 0 else dst2
                tm = None
                for j in range(8):
                    dc = g * 8 + j
                    tm = P.op("tensor", lambda e, j=j, dc=dc, src=src: e.transpose(
                        pst16[:, j * 128:(j + 1) * 128],
                        src[:, dc * 128:(dc + 1) * 128], ident16[:]),
                        TT if j == 7 else None)
                v_wait(TT, tm)
                vm = P.op("vector", lambda e, dst=dst, g=g, tt=tt: e.tensor_copy(
                    dst[:, g * 8:(g + 1) * 8, tt * 128:(tt + 1) * 128], pst16v[:]), VV)
                t_wait(VV, vm)
            G["ptok_tt"] = TT.n
            return src_done

        def gen_A1(b, xsrc_h, ss):
            for tt in range(NTT):
                xt = xtA if tt % 2 == 0 else xtB
                P.wait("sync", VV, G["xt"][tt % 2])
                P.dma("sync", LD, xt[:],
                      xsrc_h[b * TB + tt * 128: b * TB + (tt + 1) * 128, :])
                fl = P.fence("sync")
                v_wait(LQ, fl)
                G["xt"][tt % 2] = emit_ln_stats(tt, xt, ss)
                yield
            emit_ln_finish(ss)
            yield

        def gen_A2(b, xsrc_h, ss):
            v_wait(TT, G["xnp_tt"])
            for tt in range(NTT):
                xt = xtA if tt % 2 == 0 else xtB
                P.wait("sync", VV, G["xt"][tt % 2])
                P.dma("sync", LD, xt[:],
                      xsrc_h[b * TB + tt * 128: b * TB + (tt + 1) * 128, :])
                fl = P.fence("sync")
                v_wait(LQ, fl)
                G["xt"][tt % 2] = emit_split_transpose(tt, xt, True, xnp1, xnp2, ss)
                yield
            G["vv_xnp"] = VV.n

        def emit_ffn_A(b, xsrc_h, ss):
            for _ in gen_A1(b, xsrc_h, ss):
                pass
            for _ in gen_A2(b, xsrc_h, ss):
                pass

        def emit_ffn_sections(b, first_pass, aux_early=None, aux_late=None):
            early_done = [aux_early is None]
            late_done = [aux_late is None]

            def step_early():
                if not early_done[0]:
                    try:
                        next(aux_early)
                    except StopIteration:
                        early_done[0] = True

            def step_late():
                if not early_done[0]:
                    step_early()
                elif not late_done[0]:
                    try:
                        next(aux_late)
                    except StopIteration:
                        late_done[0] = True
            vv_xnp = G["vv_xnp"]
            for s in range(NSEC):
                if first_pass:
                    c0s, c1s = s * SEC, (s + 1) * SEC
                    for rch in range(4):
                        r0, r1 = rch * 512, (rch + 1) * 512
                        P.dma("gpsimd", GD, wg16[r0:r1, c0s:c1s], wg_h[r0:r1, c0s:c1s])
                        P.dma("gpsimd", GD, wu16[r0:r1, c0s:c1s], wu_h[r0:r1, c0s:c1s])
                    P.dma("gpsimd", GD, wd16[c0s:c1s, :], wd_h[c0s:c1s, :])
                    P.fence("gpsimd")
                for pair in range(8):
                    par = pair % 2
                    c0 = s * SEC + pair * 256
                    P.wait("gpsimd", TT, G["wg"][par])
                    P.dma("gpsimd", GD, wgv[:, par],
                          wg16.rearrange("(c p) i -> p c i", p=128)[:, :, c0:c0 + 256])
                    P.dma("gpsimd", GD, wuv[:, par],
                          wu16.rearrange("(c p) i -> p c i", p=128)[:, :, c0:c0 + 256])
                    f_gu = P.fence("gpsimd")
                    for ii in range(2):
                        isub = s * ISS + pair * 2 + ii
                        si = pair * 2 + ii
                        bg = pb[(si * 2) % 4]
                        bu = pb[(si * 2 + 1) % 4]
                        t_wait(GQ, f_gu)
                        if s == 0 and pair == 0 and ii == 0:
                            t_wait(VV, vv_xnp)
                            t_wait(VV, G["pbfree_vv"])
                        t_wait(SC, G["gbank"][si % 2])
                        t_wait(VV, G["ubank"][si % 2])
                        tg = tu = None
                        for dc in range(NDC):
                            for piece in range(2):
                                xp = xnp1 if piece == 0 else xnp2
                                first = (dc == 0 and piece == 0)
                                last = (dc == NDC - 1 and piece == 1)
                                tg = P.op("tensor", lambda e, bg=bg, dc=dc, xp=xp,
                                          par=par, ii=ii, f=first, l=last:
                                          e.matmul(bg[:],
                                                   wgv[:, par, dc, ii * 128:(ii + 1) * 128],
                                                   xp[:, dc, :], start=f, stop=l),
                                          TT if last else None)
                        for dc in range(NDC):
                            for piece in range(2):
                                xp = xnp1 if piece == 0 else xnp2
                                first = (dc == 0 and piece == 0)
                                last = (dc == NDC - 1 and piece == 1)
                                tu = P.op("tensor", lambda e, bu=bu, dc=dc, xp=xp,
                                          par=par, ii=ii, f=first, l=last:
                                          e.matmul(bu[:],
                                                   wuv[:, par, dc, ii * 128:(ii + 1) * 128],
                                                   xp[:, dc, :], start=f, stop=l),
                                          TT if last else None)
                        s_wait(TT, tg)
                        sg_ = P.op("scalar", lambda e, bg=bg, isub=isub, si=si:
                                   e.activation(out=gub[:, (si % 2) * 2, :], in_=bg[:],
                                                func=AF.Silu,
                                                scale=sgate[:, isub:isub + 1]), SC)
                        G["gbank"][si % 2] = sg_
                        v_wait(TT, tu)
                        v_wait(SC, sg_)
                        P.op("vector", lambda e, bu=bu, isub=isub, si=si:
                             e.tensor_scalar(gub[:, (si % 2) * 2 + 1, :], bu[:],
                                             sup[:, isub:isub + 1], None, ALU.mult))
                        P.op("vector", lambda e, si=si: e.tensor_tensor(
                            xn32[:, 0:TB], gub[:, (si % 2) * 2, :],
                            gub[:, (si % 2) * 2 + 1, :], ALU.mult))
                        P.op("vector", lambda e, si=si: e.tensor_copy(
                            fusep1[:, si, :], xn32[:, 0:TB]))
                        fv = P.op("vector", lambda e, si=si: e.tensor_tensor(
                            fusep2[:, si, :], xn32[:, 0:TB], fusep1[:, si, :],
                            ALU.subtract), VV)
                        G["ubank"][si % 2] = fv
                    G["wg"][par] = TT.n
                    step_early()
                vv_fuse = VV.n
                if s == NSEC - 1:
                    G["xnp_tt"] = TT.n
                for dc in range(NDC):
                    par = dc % 2
                    P.wait("gpsimd", TT, G["wd"][par])
                    P.dma("gpsimd", GD, wdv[par][:],
                          wd16.rearrange("(i p) d -> p i d", p=128)
                          [:, s * ISS:(s + 1) * ISS, dc * 128:(dc + 1) * 128])
                    f_wd = P.fence("gpsimd")
                    bd = pb[4 + (dc % 2)]
                    t_wait(GQ, f_wd)
                    if dc == 0:
                        t_wait(VV, vv_fuse)
                    t_wait(VV, G["dbank"][dc % 2])
                    if s == 0 and dc < 2:
                        t_wait(TT, G["acc_tt"])
                    tm = None
                    for si in range(ISS):
                        for piece in range(2):
                            fp = fusep1 if piece == 0 else fusep2
                            first = (si == 0 and piece == 0)
                            last = (si == ISS - 1 and piece == 1)
                            tm = P.op("tensor", lambda e, bd=bd, si=si, fp=fp, par=par,
                                      f=first, l=last: e.matmul(
                                          bd[:], wdv[par][:, si, :], fp[:, si, :],
                                          start=f, stop=l), TT if last else None)
                    G["wd"][par] = TT.n
                    v_wait(TT, tm)
                    if s == 0:
                        dv = P.op("vector", lambda e, bd=bd, dc=dc: e.tensor_copy(
                            acc[:, dc, :], bd[:]), VV)
                    else:
                        dv = P.op("vector", lambda e, bd=bd, dc=dc: e.tensor_tensor(
                            acc[:, dc, :], acc[:, dc, :], bd[:], ALU.add), VV)
                    G["dbank"][dc % 2] = dv
                    if s == NSEC - 1:
                        step_late()
                    else:
                        step_early()
            while not early_done[0]:
                step_early()
            while not late_done[0] and aux_late is not None:
                if not late_done[0]:
                    try:
                        next(aux_late)
                    except StopIteration:
                        late_done[0] = True

        def emit_ffn_D(b, xsrc_h, dst_h, is_ffn1, ss):
            v_wait(VV, G["dbank"][0])
            v_wait(VV, G["dbank"][1])
            vs_last = None
            for dc in range(NDC):
                vs_last = P.op("vector", lambda e, dc=dc: e.tensor_scalar(
                    acc[:, dc, :], acc[:, dc, :], sdh[:, dc:dc + 1], None, ALU.mult),
                    VV if dc == NDC - 1 else None)
            t_wait(VV, vs_last)
            tpb_g = [0, 0]
            for tt in range(NTT):
                xt = xtA if tt % 2 == 0 else xtB
                ob = obuf if tt % 2 == 0 else att32
                P.wait("sync", VV, G["xt"][tt % 2])
                P.dma("sync", LD, xt[:],
                      xsrc_h[b * TB + tt * 128: b * TB + (tt + 1) * 128, :])
                fl = P.fence("sync")
                vlast = None
                for dg in range(4):
                    bx = pb[dg % 2]
                    t_wait(VV, tpb_g[dg % 2])
                    tm = None
                    for j in range(4):
                        dc = dg * 4 + j
                        tm = P.op("tensor", lambda e, bx=bx, j=j, dc=dc, tt=tt:
                                  e.transpose(bx[:, j * 128:(j + 1) * 128],
                                              acc[:, dc, tt * 128:(tt + 1) * 128],
                                              ident[:]), TT if j == 3 else None)
                    v_wait(TT, tm)
                    v_wait(LQ, fl)
                    if dg == 0 and G["obuf_mark"][tt % 2] is not None:
                        v_wait(LQ, G["obuf_mark"][tt % 2])
                    vlast = P.op("vector", lambda e, bx=bx, dg=dg, xt=xt, ob=ob:
                                 e.tensor_tensor(ob[:, dg * TB:(dg + 1) * TB], bx[:],
                                                 xt[:, dg * TB:(dg + 1) * TB], ALU.add), VV)
                    tpb_g[dg % 2] = vlast
                G["xt"][tt % 2] = vlast
                P.wait("sync", VV, vlast)
                P.dma("sync", LD,
                      dst_h[b * TB + tt * 128: b * TB + (tt + 1) * 128, :], ob[:])
                G["obuf_mark"][tt % 2] = P.fence("sync")
                if is_ffn1:
                    gv = emit_ln_stats(tt, ob, ss, scratch=xn32)
                    G["xt"][tt % 2] = gv
            G["acc_tt"] = TT.n
            if is_ffn1:
                emit_ln_finish(ss)
                for tt in range(NTT):
                    xt = xtA if tt % 2 == 0 else xtB
                    P.wait("sync", VV, G["xt"][tt % 2])
                    P.dma("sync", LD, xt[:],
                          x1h[b * TB + tt * 128: b * TB + (tt + 1) * 128, :])
                    fl = P.fence("sync")
                    v_wait(LQ, fl)
                    v_wait(LQ, G["xn32_lq"])
                    nv = P.op("vector", lambda e, tt=tt, xt=xt, ss=ss: e.tensor_scalar(
                        xn32[:], xt[:], mcol[:, ss, tt:tt + 1], rstd[:, ss, tt:tt + 1],
                        ALU.subtract, ALU.mult), VV)
                    G["xt"][tt % 2] = nv
                    P.wait("sync", VV, nv)
                    P.dma("sync", LD,
                          ln2loc[b * TB + tt * 128: b * TB + (tt + 1) * 128, :],
                          xn32[:])
                f_ln = P.fence("sync")
                G["xn32_lq"] = f_ln
                P.wait("gpsimd", LQ, f_ln)
                P.op("gpsimd", lambda e, b=b: e.collective_compute(
                    "AllGather", ALU.bypass, replica_groups=GROUPS,
                    ins=[ln2loc[b * TB:(b + 1) * TB, :].opt()],
                    outs=[ln2all[b * 2 * TB:(b + 1) * 2 * TB, :].opt()]), AGS[b])
            G["pbfree_vv"] = VV.n

        import itertools as _it0
        emit_ffn_A(0, x_h, 0)
        for b_ in range(NB):
            gens = []
            if b_ == 0:
                gens.append(gen_wsplit())
            if b_ + 1 < NB:
                gens.append(gen_A1(b_ + 1, x_h, (b_ + 1) % 2))
                al = gen_A2(b_ + 1, x_h, (b_ + 1) % 2)
            else:
                al = None
            ae = _it0.chain(*gens) if gens else None
            emit_ffn_sections(b_, first_pass=(b_ == 0), aux_early=ae, aux_late=al)
            emit_ffn_D(b_, x_h, x1h, True, b_ % 2)

        qkv_w_guard = [TT.n, TT.n]
        pbank_guard = [G["pbfree_vv"]] * 3
        for c in range(NB):
            for r in range(2):
                tokbase = r * T + c * TB
                rowbase = c * 2 * TB + r * TB
                t_wait(AGS[c], 1)
                v_wait(AGS[c], 1)
                v_wait(TT, G["xnp_tt"])
                for tt in range(NTT):
                    xt = xtA if tt % 2 == 0 else xtB
                    P.wait("sync", VV, G["xt"][tt % 2])
                    P.wait("sync", AGS[c], 1)
                    P.dma("sync", LD, xt[:],
                          ln2all[rowbase + tt * 128: rowbase + (tt + 1) * 128, :])
                    fl = P.fence("sync")
                    v_wait(LQ, fl)
                    G["xt"][tt % 2] = emit_split_transpose(tt, xt, False, xnp1, xnp2)
                vv_xnp = VV.n
                for h in range(H):
                    for pi, (w1, w2, bias, d1, d2, tokmajor) in enumerate((
                            (wq1h, wq2h, bq_s, q1h, q2h, False),
                            (wk1h, wk2h, bk_s, k1h, k2h, False),
                            (wv1h, wv2h, bv_s, v1h, v2h, True))):
                        par = (h * 3 + pi) % 2
                        P.wait("gpsimd", TT, qkv_w_guard[par])
                        P.dma("gpsimd", GD, qkvw[par * 2][:],
                              w1[h].rearrange("(c p) q -> p c q", p=128))
                        P.dma("gpsimd", GD, qkvw[par * 2 + 1][:],
                              w2[h].rearrange("(c p) q -> p c q", p=128))
                        f_qw = P.fence("gpsimd")
                        bidx = (h * 3 + pi) % 3
                        bk_ = pb[bidx]
                        t_wait(GQ, f_qw)
                        t_wait(VV, vv_xnp)
                        t_wait(VV, pbank_guard[bidx])
                        n_mm = 0
                        tmark = None
                        for dc in range(NDC):
                            for (wv_, xv_) in ((qkvw[par * 2], xnp1), (qkvw[par * 2], xnp2),
                                               (qkvw[par * 2 + 1], xnp1)):
                                n_mm += 1
                                first = n_mm == 1
                                last = n_mm == NDC * 3
                                tmark = P.op("tensor", lambda e, bk_=bk_, wv_=wv_, xv_=xv_,
                                             dc=dc, f=first, l=last: e.matmul(
                                                 bk_[:], wv_[:, dc, :], xv_[:, dc, :],
                                                 start=f, stop=l), TT if last else None)
                        qkv_w_guard[par] = TT.n
                        v_wait(TT, tmark)
                        v_wait(LQ, G["ptok_lq"])
                        v_wait(TT, G["ptok_tt"])
                        P.op("vector", lambda e, bk_=bk_, bias=bias, h=h: e.tensor_scalar(
                            ptok1[:, 0:TB], bk_[:], bias[:, h:h + 1], None, ALU.add))
                        v_wait(LQ, G["xn32_lq"])
                        P.op("vector", lambda e, bk_=bk_, bias=bias, h=h: e.tensor_scalar(
                            xn32[:, 0:TB], bk_[:], bias[:, h:h + 1], None, ALU.add))
                        ev = P.op("vector", lambda e: e.tensor_tensor(
                            ptok2[:, 0:TB], xn32[:, 0:TB], ptok1[:, 0:TB], ALU.subtract), VV)
                        pbank_guard[bidx] = ev
                        if not tokmajor:
                            P.wait("sync", VV, ev)
                            P.dma("sync", LD, d1[h][:, tokbase:tokbase + TB], ptok1[:, 0:TB])
                            P.dma("sync", LD, d2[h][:, tokbase:tokbase + TB],
                                  ptok2[:, 0:TB])
                            G["ptok_lq"] = P.fence("sync")
                        else:
                            t_wait(VV, ev)
                            tm = None
                            for piece in range(2):
                                srcp = ptok1 if piece == 0 else ptok2
                                for j in range(4):
                                    tm = P.op("tensor", lambda e, piece=piece, j=j, srcp=srcp:
                                              e.transpose(
                                                  pst16[:, piece * 512 + j * 128:
                                                        piece * 512 + (j + 1) * 128],
                                                  srcp[:, j * 128:(j + 1) * 128], ident16[:]),
                                              TT if (piece == 1 and j == 3) else None)
                            G["ptok_tt"] = TT.n
                            v_wait(TT, tm)
                            vm = P.op("vector", lambda e: e.tensor_copy(
                                ppflat[:, 0:1024], pst16[:]), VV)
                            t_wait(VV, vm)
                            P.wait("sync", VV, vm)
                            P.dma("sync", LD,
                                  d1[h][tokbase:tokbase + TB, :].rearrange(
                                      "(j p) d -> p j d", p=128), ppx[:, 0])
                            P.dma("sync", LD,
                                  d2[h][tokbase:tokbase + TB, :].rearrange(
                                      "(j p) d -> p j d", p=128), ppx[:, 1])
                            G["ptok_lq"] = P.fence("sync")
                G["xnp_tt"] = TT.n

        if G["obuf_mark"][1] is not None:
            P.wait("scalar", LQ, G["obuf_mark"][1])
            P.wait("vector", LQ, G["obuf_mark"][1])
        kv_guard = [TT.n, TT.n]
        q_guard = TT.n
        zp_guard = TT.n
        pp_tt_guard = [TT.n, TT.n]
        pt_guard = [VV.n, VV.n]
        psum1_guard = [VV.n]
        zbuf_guard = [VV.n]
        rsb_guard = [VV.n]
        for qb in range(NQB):
            nkc = 4 * (qb + 1)
            ext = nkc * 128
            t_wait(VV, G["pbfree_vv"])
            P.wait("sync", TT, q_guard)
            for h in range(H):
                P.dma("sync", LD, qsb[:, 0, h, :], q1h[h][:, qb * TB:(qb + 1) * TB])
                P.dma("sync", LD, qsb[:, 1, h, :], q2h[h][:, qb * TB:(qb + 1) * TB])
            f_q = P.fence("sync")
            for h in range(H):
                par = h % 2
                P.wait("sync", TT, kv_guard[par])
                P.dma("sync", LD, ksb[par][:, 0, 0:ext], k1h[h][:, 0:ext])
                P.dma("sync", LD, ksb[par][:, 1, 0:ext], k2h[h][:, 0:ext])
                P.dma("sync", LD, vsb[par][:, 0, 0:nkc, :],
                      v1h[h][0:ext, :].rearrange("(c p) d -> p c d", p=128))
                P.dma("sync", LD, vsb[par][:, 1, 0:nkc, :],
                      v2h[h][0:ext, :].rearrange("(c p) d -> p c d", p=128))
                f_kv = P.fence("sync")
                t_wait(LQ, f_kv)
                t_wait(LQ, f_q)
                def emit_scores(kc):
                    slot = kc % 2
                    bS = pb[slot]
                    t_wait(VV, pt_guard[slot])
                    sm = None
                    for (wpi, qpi, f, l) in ((0, 0, True, False), (0, 1, False, False),
                                             (1, 0, False, True)):
                        sm = P.op("tensor", lambda e, bS=bS, par=par, wpi=wpi, qpi=qpi,
                                  kc=kc, h=h, f=f, l=l: e.matmul(
                                      bS[:], ksb[par][:, wpi, kc * 128:(kc + 1) * 128],
                                      qsb[:, qpi, h, :], start=f, stop=l),
                                  TT if l else None)
                    s_wait(TT, sm)
                    s_wait(VV, pt_guard[slot])
                    ex = P.op("scalar", lambda e, bS=bS, slot=slot: e.activation(
                        out=pt32[slot], in_=bS[:], func=AF.Exp, scale=INV_SCALE,
                        bias=negb[:]), SC)
                    v_wait(SC, ex)
                    v_wait(TT, pp_tt_guard[slot])
                    if kc >= qb * 4:
                        j = kc - qb * 4
                        P.op("vector", lambda e, slot=slot, j=j: e.tensor_tensor(
                            pt32[slot], pt32[slot], masks[:, j, :], ALU.mult))
                    if qb == 0 and h == 0 and kc == 0:
                        v_wait(LQ, G["ptok_lq"])
                        v_wait(TT, G["ptok_tt"])
                    P.op("vector", lambda e, slot=slot: e.tensor_copy(
                        ppb[:, slot * 2, :], pt32[slot]))
                    sp = P.op("vector", lambda e, slot=slot: e.tensor_tensor(
                        ppb[:, slot * 2 + 1, :], pt32[slot],
                        ppb[:, slot * 2, :], ALU.subtract), VV)
                    pt_guard[slot] = sp
                    return sp

                def emit_av(kc, sp):
                    slot = kc % 2
                    t_wait(VV, sp)
                    if kc == 0:
                        t_wait(VV, psum1_guard[0])
                        t_wait(VV, zbuf_guard[0])
                    last = kc == nkc - 1
                    P.op("tensor", lambda e, par=par, kc=kc, slot=slot: e.matmul(
                        pb[2][:], vsb[par][:, 0, kc, :], ppb[:, slot * 2, :],
                        start=(kc == 0), stop=False))
                    P.op("tensor", lambda e, par=par, kc=kc, slot=slot: e.matmul(
                        pb[2][:], vsb[par][:, 0, kc, :], ppb[:, slot * 2 + 1, :],
                        start=False, stop=False))
                    P.op("tensor", lambda e, par=par, kc=kc, slot=slot, l=last: e.matmul(
                        pb[2][:], vsb[par][:, 1, kc, :], ppb[:, slot * 2, :],
                        start=False, stop=l))
                    P.op("tensor", lambda e, kc=kc, slot=slot: e.matmul(
                        psum1[:], ones16[:], ppb[:, slot * 2, :],
                        start=(kc == 0), stop=False))
                    tavs = P.op("tensor", lambda e, kc=kc, slot=slot, l=last: e.matmul(
                        psum1[:], ones16[:], ppb[:, slot * 2 + 1, :],
                        start=False, stop=l), TT)
                    pp_tt_guard[slot] = tavs

                pend = None
                for kc in range(nkc):
                    sp = emit_scores(kc)
                    if pend is not None:
                        emit_av(*pend)
                    pend = (kc, sp)
                emit_av(*pend)
                kv_guard[par] = TT.n
                v_wait(TT, TT.n)
                if h == 0:
                    v_wait(TT, zp_guard)
                sc_ = P.op("vector", lambda e: e.tensor_copy(sums32[:], psum1[:]), VV)
                v_wait(VV, sc_)
                rv = P.op("vector", lambda e: e.reciprocal(rcp[:], sums32[:]), VV)
                psum1_guard[0] = rv
                t_wait(VV, rv)
                t_wait(VV, rsb_guard[0])
                bc = P.op("tensor", lambda e: e.matmul(
                    pb[3][:], ones32[:], rcp[:], start=True, stop=True), TT)
                v_wait(TT, bc)
                rsb_guard[0] = P.op("vector", lambda e: e.tensor_copy(rsbv, pb[3][:]), VV)
                zbuf_guard[0] = P.op("vector", lambda e: e.tensor_tensor(
                    zbuf, pb[2][:], rsbv, ALU.mult), VV)
                P.op("vector", lambda e, h=h: e.tensor_copy(zpv[:, 0, h, :], zbuf))
                zv = P.op("vector", lambda e, h=h: e.tensor_tensor(
                    zpv[:, 1, h, :], zbuf, zpv[:, 0, h, :], ALU.subtract), VV)
            wo_guard = [TT.n, TT.n] if qb == 0 else wo_guard
            obank_guard = [0, 0]
            for dmc in range(NDC):
                par = dmc % 2
                P.wait("sync", TT, wo_guard[par])
                P.dma("sync", LD, wov[par][:, 0],
                      wo1h.rearrange("h p d -> p h d")[:, :, dmc * 128:(dmc + 1) * 128])
                P.dma("sync", LD, wov[par][:, 1],
                      wo2h.rearrange("h p d -> p h d")[:, :, dmc * 128:(dmc + 1) * 128])
                f_wo = P.fence("sync")
                bo_ = pb[4 + (dmc % 2)]
                t_wait(LQ, f_wo)
                t_wait(VV, obank_guard[dmc % 2])
                if dmc < 2:
                    t_wait(TT, G["acc_tt"])
                n_mm = 0
                tmark = None
                for h in range(H):
                    for (wpi, zpi) in ((0, 0), (0, 1), (1, 0)):
                        n_mm += 1
                        first = n_mm == 1
                        last = n_mm == H * 3
                        tmark = P.op("tensor", lambda e, bo_=bo_, par=par, wpi=wpi,
                                     zpi=zpi, h=h, f=first, l=last: e.matmul(
                                         bo_[:], wov[par][:, wpi, h, :], zpv[:, zpi, h, :],
                                         start=f, stop=l), TT if last else None)
                wo_guard[par] = TT.n
                v_wait(TT, tmark)
                if dmc < 2:
                    v_wait(TT, G["acc_tt"])
                ov = P.op("vector", lambda e, bo_=bo_, dmc=dmc: e.tensor_scalar(
                    acc[:, dmc, :], bo_[:], bo_half[:, dmc:dmc + 1], None, ALU.add), VV)
                obank_guard[dmc % 2] = ov
            zp_guard = TT.n
            q_guard = TT.n
            t_wait(SC, SC.n)
            t_wait(VV, VV.n)
            half_g, cc_ = qb // 4, qb % 4
            tpb_g = [0, 0]
            for tt in range(NTT):
                ob = obuf if tt % 2 == 0 else xn32
                vlast = None
                for dg in range(4):
                    bx = pb[dg % 2]
                    t_wait(VV, tpb_g[dg % 2])
                    tm = None
                    for j in range(4):
                        dc = dg * 4 + j
                        tm = P.op("tensor", lambda e, bx=bx, j=j, dc=dc, tt=tt:
                                  e.transpose(bx[:, j * 128:(j + 1) * 128],
                                              acc[:, dc, tt * 128:(tt + 1) * 128],
                                              ident[:]), TT if j == 3 else None)
                    v_wait(TT, tm)
                    if dg == 0:
                        if tt % 2 == 0 and G["obuf_mark"][0] is not None:
                            v_wait(LQ, G["obuf_mark"][0])
                        if tt % 2 == 1:
                            v_wait(LQ, G["xn32_lq"])
                    vlast = P.op("vector", lambda e, bx=bx, dg=dg, ob=ob: e.tensor_copy(
                        ob[:, dg * TB:(dg + 1) * TB], bx[:]), VV)
                    tpb_g[dg % 2] = vlast
                P.wait("sync", VV, vlast)
                srow = cc_ * 2 * TB + half_g * TB + tt * 128
                P.dma("sync", LD, rsin[srow:srow + 128, :], ob[:])
                fmk = P.fence("sync")
                if tt % 2 == 0:
                    G["obuf_mark"][0] = fmk
                else:
                    G["xn32_lq"] = fmk
            G["acc_tt"] = TT.n
            G["pbfree_vv"] = VV.n
            f_stage = P.fence("sync")
            if qb >= 4:
                cdone = qb - 4
                P.wait("gpsimd", LQ, f_stage)
                P.op("gpsimd", lambda e, c=cdone: e.collective_compute(
                    "ReduceScatter", ALU.add, replica_groups=GROUPS,
                    ins=[rsin[c * 2 * TB:(c + 1) * 2 * TB, :].opt()],
                    outs=[rsout[c * TB:(c + 1) * TB, :].opt()]), RSS[cdone])

        G["wg"] = [TT.n, TT.n]
        G["wd"] = [TT.n, TT.n]
        G["xnp_tt"] = TT.n
        G["gbank"] = [SC.n, SC.n]
        G["ubank"] = [VV.n, VV.n]
        G["dbank"] = [VV.n, VV.n]
        def gen_x2_assembly(c):
            v_wait(RSS[c], 1)
            for tt in range(NTT):
                P.wait("sync", RSS[c], 1)
                P.wait("sync", VV, G["xt"][0])
                P.wait("sync", VV, G["xt"][1])
                P.dma("sync", LD, xtA[:],
                      rsout[c * TB + tt * 128: c * TB + (tt + 1) * 128, :])
                P.dma("sync", LD, xtB[:],
                      x1h[c * TB + tt * 128: c * TB + (tt + 1) * 128, :])
                fl = P.fence("sync")
                v_wait(LQ, fl)
                v_wait(VV, VV.n)
                av = P.op("vector", lambda e: e.tensor_tensor(
                    xtA[:], xtA[:], xtB[:], ALU.add), VV)
                G["xt"][0] = av
                G["xt"][1] = av
                P.wait("sync", VV, av)
                P.dma("sync", LD, x2h[c * TB + tt * 128: c * TB + (tt + 1) * 128, :], xtA[:])
                yield
            P.fence("sync")

        for _ in gen_x2_assembly(0):
            pass
        emit_ffn_A(0, x2h, 0)
        import itertools as _it
        for c in range(NB):
            if c + 1 < NB:
                ae = _it.chain(gen_x2_assembly(c + 1), gen_A1(c + 1, x2h, (c + 1) % 2))
                al = gen_A2(c + 1, x2h, (c + 1) % 2)
            else:
                ae = al = None
            emit_ffn_sections(c, first_pass=False, aux_early=ae, aux_late=al)
            emit_ffn_D(c, x2h, y_h, False, c % 2)

        if dbg:
            d_x1 = nc.dram_tensor("d_x1", [T, D], F32, kind="ExternalOutput").ap()
            d_l2 = nc.dram_tensor("d_l2", [S, D], F32, kind="ExternalOutput").ap()
            d_k1 = nc.dram_tensor("d_k1", [H, DH, S], F16, kind="ExternalOutput").ap()
            d_ri = nc.dram_tensor("d_ri", [S, D], F32, kind="ExternalOutput").ap()
            d_ro = nc.dram_tensor("d_ro", [T, D], F32, kind="ExternalOutput").ap()
            d_bo = nc.dram_tensor("d_bo", [128, D], F32, kind="ExternalOutput").ap()
            d_x2 = nc.dram_tensor("d_x2", [T, D], F32, kind="ExternalOutput").ap()
            for dst, srcd in ((d_x1, x1h), (d_l2, ln2all), (d_k1, k1h),
                              (d_ri, rsin), (d_ro, rsout), (d_x2, x2h)):
                ss = srcd.rearrange("a b -> (a b)") if len(srcd.shape) == 2 else \
                    srcd.rearrange("a b c -> (a b c)")
                dd = dst.rearrange("a b -> (a b)") if len(dst.shape) == 2 else \
                    dst.rearrange("a b c -> (a b c)")
                n = ss.shape[0]
                step = n // 8
                for i in range(8):
                    P.dma("sync", LD, dd[i * step:(i + 1) * step], ss[i * step:(i + 1) * step])
            P.dma("sync", LD, d_bo[:], bo_bc[:])

        P.fence("sync")
        P.fence("gpsimd")

        @block.sync
        def _(engq):
            for fn in P.q["sync"]:
                fn(engq)

        @block.gpsimd
        def _(engq):
            for fn in P.q["gpsimd"]:
                fn(engq)

        @block.tensor
        def _(engq):
            for fn in P.q["tensor"]:
                fn(engq)

        @block.vector
        def _(engq):
            for fn in P.q["vector"]:
                fn(engq)

        @block.scalar
        def _(engq):
            for fn in P.q["scalar"]:
                fn(engq)

    return nc


def _host_inputs(inputs):
    x = np.ascontiguousarray(np.asarray(inputs["L0_attn_x"], dtype=np.float32))
    WQ = np.asarray(inputs["L0_attn_W_Q"], dtype=np.float32)
    WK = np.asarray(inputs["L0_attn_W_K"], dtype=np.float32)
    WV = np.asarray(inputs["L0_attn_W_V"], dtype=np.float32)
    WO = np.asarray(inputs["L0_attn_W_O"], dtype=np.float32)
    bQ = np.asarray(inputs["L0_attn_b_Q"], dtype=np.float32)
    bK = np.asarray(inputs["L0_attn_b_K"], dtype=np.float32)
    bV = np.asarray(inputs["L0_attn_b_V"], dtype=np.float32)
    bO = np.asarray(inputs["L0_attn_b_O"], dtype=np.float32)
    wg = np.ascontiguousarray(np.asarray(inputs["L0_ffn_w_gate"], dtype=np.int8))
    wu = np.ascontiguousarray(np.asarray(inputs["L0_ffn_w_up"], dtype=np.int8))
    wd = np.ascontiguousarray(np.asarray(inputs["L0_ffn_w_down"], dtype=np.int8))
    sg = np.asarray(inputs["L0_ffn_s_gate"], dtype=np.float32)
    su = np.asarray(inputs["L0_ffn_s_up"], dtype=np.float32)
    sd = np.asarray(inputs["L0_ffn_s_down"], dtype=np.float32)

    sg_r = np.ascontiguousarray(sg.reshape(NIS, 128).T)
    su_r = np.ascontiguousarray(su.reshape(NIS, 128).T)
    sd_r = np.ascontiguousarray(sd.reshape(NDC, 128).T)
    bo_r = np.ascontiguousarray(bO.reshape(NDC, 128).T)
    jj = np.arange(4)[:, None, None]
    pp_ = np.arange(128)[None, :, None]
    ff = np.arange(TB)[None, None, :]
    masks = (pp_ <= ff - 128 * jj).astype(np.float16)
    masks = np.ascontiguousarray(masks.transpose(1, 0, 2))
    ident = np.eye(128, dtype=np.float32)
    ident16 = np.eye(128, dtype=np.float16)
    ones16 = np.ones((128, 1), dtype=np.float16)
    ones32 = np.ones((1, 128), dtype=np.float32)

    in_maps = []
    for c in range(8):
        b, half = c // 2, c % 2
        hs = slice(half * H, (half + 1) * H)
        in_maps.append({
            "x_my": np.ascontiguousarray(x[b, half * T:(half + 1) * T, :]),
            "w_gate": wg, "w_up": wu, "w_down": wd,
            "s_gate_r": sg_r, "s_up_r": su_r, "s_down_r": sd_r,
            "wq": np.ascontiguousarray(WQ[hs]), "wk": np.ascontiguousarray(WK[hs]),
            "wv": np.ascontiguousarray(WV[hs]), "wo": np.ascontiguousarray(WO[hs]),
            "bq_r": np.ascontiguousarray(bQ[hs].T),
            "bk_r": np.ascontiguousarray(bK[hs].T),
            "bv_r": np.ascontiguousarray(bV[hs].T),
            "bo_r": bo_r, "masks": masks, "ident": ident, "ident16": ident16,
            "ones16": ones16, "ones32": ones32,
        })
    return in_maps


def kernel(**inputs):
    in_maps = _host_inputs(inputs)
    nc = build_program()
    res = run_bass_kernel_spmd(nc, in_maps, core_ids=list(range(8)))
    y = np.empty((B, S, D), dtype=np.float32)
    for c in range(8):
        b, half = c // 2, c % 2
        y[b, half * T:(half + 1) * T, :] = np.asarray(res.results[c]["y_my"])
    return y
